# version 14
# baseline (speedup 1.0000x reference)
import numpy as np
from contextlib import ExitStack
import concourse.bass as bass
import concourse.mybir as mybir
from concourse.bass_utils import run_bass_kernel_spmd

F32 = mybir.dt.float32
BF16 = mybir.dt.bfloat16
AF = mybir.ActivationFunctionType
ALU = mybir.AluOpType
AX = mybir.AxisListType
C32 = float(1.5 * 2.0**23)
C64 = float(1.5 * 2.0**52)

B, L, D, N, H = 4, 8192, 1024, 64, 4096
NCORE = 8
TOK = L // 2
NCH = TOK // 128
NIT = 2 * NCH
INV_D = 1.0 / D
W1_TILES = 32
W2_TILES = 32
NWT = W1_TILES + W2_TILES

_PT_ORDER = [("x", 0)]
for _b in range(NIT + 2):
    if _b + 1 < NIT:
        _PT_ORDER.append(("x", _b + 1))
    if 0 <= _b - NCH < NCH:
        _PT_ORDER.append(("k", _b - NCH))
PT_IDX = {key: i for i, key in enumerate(_PT_ORDER)}

W_SLOT = []
_rc = [0, 0, 0]
for _t in range(2 * NWT):
    _r = (_t % NWT) % 3
    W_SLOT.append((_r, _rc[_r]))
    _rc[_r] += 1
W_RING = [[t for t in range(2 * NWT) if W_SLOT[t][0] == r] for r in range(3)]


def build_nc():
    nc = bass.Bass("TRN2", target_bir_lowering=False)
    dr = {}
    for n, shp in [("xh", (TOK, D)), ("xo", (TOK, D)),
                   ("w1", (H, D)), ("w2", (D, H)),
                   ("win", (N, D)), ("wout", (D, N)),
                   ("logA", (N, 1)), ("Bp", (N, 1)), ("Cp", (N, 1)),
                   ("logdt", (1, 1)), ("gate", (1, 1))]:
        dr[n] = nc.dram_tensor(n, list(shp), F32, kind="ExternalInput").ap()
    dr["out"] = nc.dram_tensor("out", [TOK, D], F32, kind="ExternalOutput").ap()

    sbt = lambda n, shp, dt=F32: nc.alloc_sbuf_tensor(n, list(shp), dt).ap()
    t1T = sbt("t1T", (128, 8 * H), BF16)
    t2T = sbt("t2T", (128, 32 * D), BF16)
    winT = sbt("winT", (128, 8 * N), BF16)
    woct = sbt("woct", (64, D), BF16)
    xb = [sbt(f"xb{i}", (128, D)) for i in range(2)]
    xn = sbt("xn", (128, D), BF16)
    xnfm = [sbt(f"xnfm{i}", (128, D), BF16) for i in range(2)]
    x2_3 = [sbt(f"x2_{i}", (128, D)) for i in range(3)]
    x2 = None
    xn2 = sbt("xn2", (128, D))
    k1s0 = sbt("k1s0", (128, D), BF16)
    k1s = [k1s0, k1s0]
    k1fm0 = sbt("k1fm0", (128, D), BF16)
    k1fm = [k1fm0, k1fm0]
    h_sb = sbt("h_sb", (128, H))
    vs2 = sbt("vs2", (128, 2048))
    x2 = x2_3 + [vs2[:, 1024:2048]]
    k2 = sbt("k2", (128, H), BF16)
    sc2bc = sbt("sc2bc", (128, 128))
    hred = sbt("hred", (128, 128))
    rowf = sbt("rowf", (1, 128))
    out_s0 = sbt("out_s0", (128, D))
    out_s = [out_s0, out_s0]
    u_s = [sbt(f"u_s{i}", (64, 128)) for i in range(2)]
    h_s = [sbt(f"h_s{i}", (64, 128)) for i in range(2)]
    hb_s = sbt("hb_s", (64, 128), BF16)
    abt = sbt("abt", (64, 128))
    cgt = sbt("cgt", (64, 1))
    zero64 = sbt("zero64", (64, 1))
    identb = sbt("identb", (128, 128), BF16)
    ident32 = sbt("ident32", (128, 128))
    ones1f = sbt("ones1f", (1, 128))
    TLn = ["sm", "sq", "va", "vb", "vc", "sqr", "inv", "nb", "am", "cl", "rr",
           "sc1", "a1t", "am2", "cl2", "r2", "sc2", "a2t", "C_t", "nC_t",
           "one_t", "chi", "c64", "nc64", "r1b", "s1o", "r2b", "s2o", "wcol",
           "vc2", "sqr2", "a2u", "sm2", "sq2u"]
    TLblk = sbt("TLblk", (128, len(TLn)), F32)
    TL = {n: TLblk[:, i:i + 1] for i, n in enumerate(TLn)}
    P64n = ["logA", "Bp", "Cp", "expA", "Aneg", "dtb", "gate64", "lnab",
            "Abar", "am1", "mask", "d1", "e1", "den", "rden", "f0", "f1",
            "f2", "f3", "frac", "Bbar"]
    P64blk = sbt("P64blk", (64, len(P64n)), F32)
    P64 = {n: P64blk[:, i:i + 1] for i, n in enumerate(P64n)}
    sc11 = sbt("sc11", (1, 1))
    sc12 = sbt("sc12", (1, 1))
    gt11 = sbt("gt11", (1, 1))
    pack = sbt("pack", (1, 4))
    wtot = sbt("wtot", (1, 1))
    onesn = sc2bc[0:64, 0:128]
    identD = hred
    k1 = vs2.bitcast(BF16)[:, 2048:3072]
    scr_sq = xn2
    scr_vt = vs2[:, 0:1024]
    scr_sq = xn2
    wst = [vs2[:, 0:1024], vs2[:, 1024:2048]]
    wstB = [h_sb[:, 1024:2048], h_sb[:, 2048:3072]]
    wstC = [h_sb[:, 3072:4096], out_s0[:, 0:1024]]
    WSTG = [wst, wstB, wstC]
    wqb = [k2[:, 0:1024], k2[:, 1024:2048]]
    winB = k2[:, 2048:3072]
    woctB0 = k2[:, 3072:4096]
    wost = h_sb[:, 0:512]
    winF = xn2
    wmul = x2_3[0]
    wrnd = x2_3[1]
    wabs = hred
    pT = nc.alloc_psum_tensor("pT", [128, 1024], BF16).ap()
    pmisc = nc.alloc_psum_tensor("pmisc", [128, 512], F32).ap()
    pssm = nc.alloc_psum_tensor("pssm", [128, 1024], F32).ap()
    pm1a = nc.alloc_psum_tensor("pm1a", [128, 512], F32).ap()
    pm1b = nc.alloc_psum_tensor("pm1b", [128, 512], F32).ap()
    pout = nc.alloc_psum_tensor("pout", [128, 1024], F32).ap()

    t1T_kv = t1T.rearrange("p (k r) -> p k r", k=8)
    t2T_qv = t2T.rearrange("p (q f r) -> p q f r", q=4, f=8)
    pT_v8 = pT.rearrange("p (k t) -> p k t", k=8)

    SEMS = ["lx", "lw", "lws", "lwg", "lp", "lo", "gp", "prep",
            "wa", "s1b", "s2b", "q1", "q1T", "q1e",
            "zoh1", "zoh2", "k1m", "k1me", "winc", "winTd", "winTe",
            "wotc", "wotTd", "wotTe", "woctr", "wcT1", "wcT2", "sc1m", "sc2m",
            "sq1a", "sq2a", "r1d", "r2d", "sqr1q", "sqr1a", "xn_", "xnT", "xnfm_", "umm", "scan",
            "hb", "wout", "x2_", "sqr2q", "sqr2a", "uev",
            "k1s_", "k1T", "k1fm_", "mm1", "gelu", "hmax", "hmaxT", "sc2_",
            "rowT", "rowe", "bcmm", "bce", "k2q", "m2h", "outv", "od", "pTe"]

    with ExitStack() as ctx:
        S = {n: ctx.enter_context(nc.semaphore(n)) for n in SEMS}
        block = ctx.enter_context(nc.Block())

        WSEM = ["lw", "lws", "lwg"]
        w1_v = dr["w1"].rearrange("(j p) d -> j p d", p=128)
        w2_v = dr["w2"].rearrange("(i p) (q d) -> i q p d", p=128, d=1024)

        def wtile(j):
            return w1_v[j] if j < W1_TILES else \
                w2_v[(j - W1_TILES) // 4, (j - W1_TILES) % 4]

        def w_dma(eng, r, t):
            _, si = W_SLOT[t]
            if si >= 2:
                tp = W_RING[r][si - 2]
                if tp < NWT:
                    eng.wait_ge(S["wa"], tp + 1)
                else:
                    eng.wait_ge(S["q1"], tp - NWT + 1)
            eng.wait_ge(S[WSEM[r]], 16 * si)
            eng.dma_start(out=WSTG[r][si % 2],
                          in_=wtile(t % NWT)).then_inc(S[WSEM[r]], 16)

        @block.sync
        def _(sync):
            for di, nm in enumerate(["logA", "Bp", "Cp", "logdt", "gate", "win"]):
                dst = {"logA": P64["logA"], "Bp": P64["Bp"], "Cp": P64["Cp"],
                       "logdt": sc11, "gate": gt11}.get(nm)
                d = winF[0:64, :] if nm == "win" else dst[:]
                sync.wait_ge(S["lp"], 16 * di)
                sync.dma_start(out=d, in_=dr[nm][:, :]).then_inc(S["lp"], 16)
            wo_v = dr["wout"].rearrange("(a p) n -> a p n", p=128)
            for a in range(8):
                sync.wait_ge(S["lo"], 16 * a)
                sync.dma_start(out=wost[:, a * 64:(a + 1) * 64],
                               in_=wo_v[a]).then_inc(S["lo"], 16)
            for t in W_RING[1]:
                w_dma(sync, 1, t)
            for it in range(NIT):
                sync.wait_ge(S["lx"], 16 * it)
                if it >= 2:
                    sync.wait_ge(S["xn_"], it - 1)
                    sync.wait_ge(S["sq1a"], it - 1)
                    if it - 2 >= NCH:
                        sync.wait_ge(S["x2_"], it - 2 - NCH + 1)
                    else:
                        sync.wait_ge(S["scan"], it - 1)
                src = dr["xh"] if it < NCH else dr["xo"]
                c = it if it < NCH else it - NCH
                sync.dma_start(out=xb[it % 2][:],
                               in_=src[c * 128:(c + 1) * 128, :]).then_inc(S["lx"], 16)

        @block.gpsimd
        def _(g):
            g.iota(identD[:], pattern=[[-1, 128]], base=0, channel_multiplier=1,
                   allow_small_or_imprecise_dtypes=True)
            g.drain().then_inc(S["gp"], 1)
            for t in W_RING[2]:
                w_dma(g, 2, t)
            def gp_sq2(it):
                oi = it - NCH
                g.wait_ge(S["x2_"], oi + 1)
                if it >= NCH + 1:
                    g.wait_ge(S["k1s_"], it - NCH)
                g.tensor_tensor(scr_sq[:], x2[oi % 4][:], x2[oi % 4][:],
                                ALU.mult)
                g.drain().then_inc(S["sq2a"], 1)

            def gp_sq1(it):
                g.wait_ge(S["lx"], 16 * (it + 1))
                if it >= 1:
                    g.wait_ge(S["r1d"], it)
                if it - NCH >= 1:
                    g.wait_ge(S["r2d"], it - NCH)
                g.tensor_tensor(scr_sq[:], xb[it % 2][:], xb[it % 2][:],
                                ALU.mult)
                g.drain().then_inc(S["sq1a"], 1)

            def gp_k2(oi):
                for q in range(4):
                    if q == 0:
                        g.wait_ge(S["bce"], oi + 1)
                        g.wait_ge(S["k1s_"], min(oi + 3, NCH))
                        g.wait_ge(S["r1d"], min(oi + NCH + 4, NIT))
                        g.wait_ge(S["r2d"], min(oi + 3, NCH))
                    if oi >= 1:
                        g.wait_ge(S["m2h"], 4 * (oi - 1) + q + 1)
                    hs = h_sb[:, q * 1024:(q + 1) * 1024].rearrange(
                        "p (m t) -> p m t", m=8)
                    sj = scr_sq.rearrange("p (m t) -> p m t", m=8)
                    bcb = sc2bc.unsqueeze(1).broadcast_to([128, 8, 128])
                    g.tensor_tensor(sj, hs, bcb, ALU.mult)
                    g.drain()
                    g.tensor_scalar(k2[:, q * 1024:(q + 1) * 1024], scr_sq[:],
                                    TL["C_t"][:, 0:1], TL["nC_t"][:, 0:1],
                                    op0=ALU.add, op1=ALU.add)
                    g.drain().then_inc(S["k2q"], 1)

            gp_sq1(0)
            gp_sq1(1)
            for b in range(NIT + 3):
                oi1 = b - NCH - 1
                oi3 = b - NCH - 3
                if NCH <= b + 1 < NIT:
                    gp_sq2(b + 1)
                if b + 2 < NIT:
                    gp_sq1(b + 2)
                if 0 <= oi3 < NCH:
                    g.wait_ge(S["od"], 16 * oi3)
                    g.wait_ge(S["outv"], oi3 + 1)
                    g.dma_start(out=dr["out"][oi3 * 128:(oi3 + 1) * 128, :],
                                in_=out_s[oi3 % 2][:]).then_inc(S["od"], 16)
                if 0 <= oi1 < NCH:
                    gp_k2(oi1)
            g.wait_ge(S["od"], 16 * NCH)

        @block.scalar
        def _(s):
            s.wait_ge(S["lp"], 16 * 6)
            s.activation(P64["expA"][:], P64["logA"][:], AF.Exp, bias=0.0, scale=1.0)
            s.activation(sc12[:], sc11[:], AF.Exp, bias=0.0, scale=1.0)
            s.drain().then_inc(S["zoh1"], 1)
            s.wait_ge(S["k1m"], 2)
            s.copy(P64["dtb"][:], pmisc[0:64, 0:1])
            s.copy(P64["gate64"][:], pmisc[0:64, 1:2])
            s.drain().then_inc(S["k1me"], 1)
            s.wait_ge(S["zoh2"], 1)
            s.activation(P64["Abar"][:], P64["lnab"][:], AF.Exp, bias=0.0, scale=1.0)
            s.drain().then_inc(S["zoh2"], 1)
            s.wait_ge(S["winTd"], 1)
            s.copy(winT[:], pT[:, 0:512])
            s.drain().then_inc(S["winTe"], 1)
            s.wait_ge(S["wotTd"], 1)
            s.copy(woctB0[0:64, :], pT[0:64, 0:1024])
            s.drain().then_inc(S["wotTe"], 1)
            for t in W_RING[0]:
                if t < NWT:
                    w_dma(s, 0, t)
            s.wait_ge(S["sc1m"], 2)
            s.copy(TL["r1b"][:], pmisc[:, 2:3])
            s.copy(TL["s1o"][:], pmisc[:, 3:4])
            s.drain().then_inc(S["s1b"], 1)

            def evac_ternary(jm):
                s.wait_ge(S["q1T"], jm + 1)
                if jm < W1_TILES:
                    s.copy(t1T_kv[:, :, jm * 128:(jm + 1) * 128], pT_v8[:])
                else:
                    ii, qq = (jm - W1_TILES) // 4, (jm - W1_TILES) % 4
                    s.copy(t2T_qv[:, qq:qq + 1, :, ii * 128:(ii + 1) * 128],
                           pT_v8.unsqueeze(1))
                s.drain().then_inc(S["q1e"], 1)

            for j in range(NWT):
                if j == W1_TILES:
                    s.wait_ge(S["sc2m"], 2)
                    s.copy(TL["r2b"][:], pmisc[:, 4:5])
                    s.copy(TL["s2o"][:], pmisc[:, 5:6])
                    s.drain().then_inc(S["s2b"], 1)
                if j % 3 == 0:
                    w_dma(s, 0, NWT + j)
                if j >= 1:
                    evac_ternary(j - 1)
            evac_ternary(NWT - 1)

            def se_hb(it):
                s.wait_ge(S["scan"], it + 1)
                s.copy(hb_s[:], h_s[it % 2][:])
                s.drain().then_inc(S["hb"], 1)

            def se_k1fm(oi):
                s.wait_ge(S["k1T"], oi + 1)
                if oi >= 1:
                    s.wait_ge(S["mm1"], 8 * oi)
                s.copy(k1fm[oi % 2][:], pT[:])
                s.drain().then_inc(S["k1fm_"], 1)
                s.drain().then_inc(S["pTe"], 1)

            def se_gelu(oi, g):
                s.wait_ge(S["mm1"], 8 * oi + g + 1)
                if oi >= 1:
                    s.wait_ge(S["k2q"], 4 * (oi - 1) + g // 2 + 1)
                    if g == 0:
                        s.wait_ge(S["hmax"], oi)
                if oi == 0 and g == 2:
                    s.wait_ge(S["q1"], NWT)
                bank = pm1a if g % 2 == 0 else pm1b
                s.activation(h_sb[:, g * 512:(g + 1) * 512], bank[:], AF.Gelu,
                             bias=0.0, scale=1.0)
                s.drain().then_inc(S["gelu"], 1)

            def se_copy2(it):
                oi = it - NCH
                s.wait_ge(S["sq2a"], oi + 1)
                s.wait_ge(S["xnT"], it + 1)
                s.activation(xn[:], scr_sq[:], AF.Copy, bias=0.0, scale=1.0,
                             accum_out=TL["sq2u"][:, 0:1])
                s.drain().then_inc(S["r2d"], 1)

            def se_copy1(it):
                s.wait_ge(S["sq1a"], it + 1)
                if it >= 1:
                    s.wait_ge(S["xnT"], it)
                s.activation(xn[:], scr_sq[:], AF.Copy, bias=0.0, scale=1.0,
                             accum_out=TL["sq"][:, 0:1])
                s.drain().then_inc(S["r1d"], 1)

            def se_sqrt2(it):
                oi = it - NCH
                s.wait_ge(S["sqr2q"], oi + 1)
                s.activation(TL["sqr2"][:], TL["vc2"][:], AF.Sqrt, bias=0.0,
                             scale=1.0)
                s.drain().then_inc(S["sqr2a"], 1)

            def se_sqrt1(it):
                s.wait_ge(S["sqr1q"], it + 1)
                s.activation(TL["sqr"][:], TL["vc"][:], AF.Sqrt, bias=0.0,
                             scale=1.0)
                s.drain().then_inc(S["sqr1a"], 1)

            def se_xnfm(it):
                s.wait_ge(S["xnT"], it + 1)
                if it >= 2:
                    s.wait_ge(S["umm"], it - 1)
                s.copy(xnfm[it % 2][:], pT[:])
                s.drain().then_inc(S["xnfm_"], 1)
                s.drain().then_inc(S["pTe"], 1)

            def se_rowbc(oi):
                s.wait_ge(S["rowT"], oi + 1)
                s.copy(rowf[:], pmisc[0:1, 256:384])
                s.drain().then_inc(S["rowe"], 1)
                s.wait_ge(S["bcmm"], oi + 1)
                s.copy(sc2bc[:], pmisc[:, 384:512])
                s.drain().then_inc(S["bce"], 1)

            se_copy1(0)
            se_sqrt1(0)
            se_copy1(1)
            se_sqrt1(1)
            se_xnfm(0)
            for b in range(NIT + 2):
                oi1 = b - NCH - 1
                has1 = 0 <= oi1 < NCH
                if b + 1 < NIT:
                    se_xnfm(b + 1)
                if NCH <= b + 1 < NIT:
                    se_hb(b + 1)
                if has1:
                    for g in range(0, 4):
                        se_gelu(oi1, g)
                if NCH <= b + 1 < NIT:
                    se_copy2(b + 1)
                if has1:
                    se_gelu(oi1, 4)
                if b + 2 < NIT:
                    se_copy1(b + 2)
                if has1:
                    se_gelu(oi1, 5)
                if NCH <= b + 1 < NIT:
                    se_sqrt2(b + 1)
                if b + 2 < NIT:
                    se_sqrt1(b + 2)
                if has1:
                    se_gelu(oi1, 6)
                    se_gelu(oi1, 7)
                if 0 <= b - NCH < NCH:
                    se_k1fm(b - NCH)
                if has1:
                    se_rowbc(oi1)

        @block.vector
        def _(v):
            D_ = v.drain
            v.wait_ge(S["gp"], 1)
            v.tensor_single_scalar(identb[:], identD[:], 0.0, ALU.is_equal)
            v.tensor_single_scalar(ident32[:], identD[:], 0.0, ALU.is_equal)
            v.memset(ones1f[:], 1.0)
            v.memset(onesn[:], 1.0)
            v.memset(zero64[:], 0.0)
            v.memset(TL["one_t"][:], 1.0)
            v.memset(TL["c64"][:], C64)
            v.memset(TL["nc64"][:], -C64)
            D_()
            v.tensor_scalar(TL["chi"][:], TL["one_t"][:], TL["c64"][:, 0:1],
                            TL["nc64"][:, 0:1], op0=ALU.add, op1=ALU.add); D_()
            v.tensor_scalar(TL["va"][:], TL["chi"][:], TL["c64"][:, 0:1], C32,
                            op0=ALU.mult, op1=ALU.add); D_()
            v.tensor_copy(TL["C_t"][:], TL["va"][:]); D_()
            v.tensor_scalar(TL["nC_t"][:], TL["va"][:], -1.0, None, op0=ALU.mult)
            D_().then_inc(S["prep"], 1)
            v.wait_ge(S["zoh1"], 1)
            v.tensor_scalar(P64["Aneg"][:], P64["expA"][:], -1.0, None,
                            op0=ALU.mult); D_()
            v.tensor_copy(pack[0:1, 0:1], sc12[:])
            v.tensor_copy(pack[0:1, 1:2], gt11[:])
            D_().then_inc(S["k1m"], 1)
            v.wait_ge(S["k1me"], 1)
            v.tensor_tensor(P64["lnab"][:], P64["Aneg"][:], P64["dtb"][:],
                            ALU.mult)
            D_().then_inc(S["zoh2"], 1)
            v.wait_ge(S["zoh2"], 2)
            v.tensor_scalar(P64["am1"][:], P64["Abar"][:], -1.0, None,
                            op0=ALU.add); D_()
            v.tensor_single_scalar(P64["mask"][:], P64["expA"][:], 1e-6,
                                   ALU.is_lt); D_()
            v.tensor_tensor(P64["d1"][:], P64["mask"][:], P64["Aneg"][:],
                            ALU.mult); D_()
            v.tensor_tensor(P64["e1"][:], P64["Aneg"][:], P64["d1"][:],
                            ALU.subtract); D_()
            v.tensor_tensor(P64["den"][:], P64["e1"][:], P64["mask"][:],
                            ALU.add); D_()
            v.reciprocal(P64["rden"][:], P64["den"][:]); D_()
            v.tensor_tensor(P64["f0"][:], P64["am1"][:], P64["rden"][:],
                            ALU.mult); D_()
            v.tensor_tensor(P64["f1"][:], P64["mask"][:], P64["dtb"][:],
                            ALU.mult); D_()
            v.tensor_tensor(P64["f2"][:], P64["mask"][:], P64["f0"][:],
                            ALU.mult); D_()
            v.tensor_tensor(P64["f3"][:], P64["f0"][:], P64["f2"][:],
                            ALU.subtract); D_()
            v.tensor_tensor(P64["frac"][:], P64["f3"][:], P64["f1"][:],
                            ALU.add); D_()
            v.tensor_tensor(P64["Bbar"][:], P64["frac"][:], P64["Bp"][:],
                            ALU.mult); D_()
            v.tensor_single_scalar(abt[:], onesn[:], P64["Abar"][:, 0:1],
                                   ALU.mult); D_()
            v.wait_ge(S["lp"], 16 * 6)
            v.wait_ge(S["lo"], 16 * 8)
            v.tensor_copy(winB[0:64, :], winF[0:64, :])
            D_().then_inc(S["winc"], 1)
            v.tensor_copy(wqb[0][:, 0:512], wost[:, 0:512])
            D_().then_inc(S["wotc"], 1)
            v.wait_ge(S["wotTe"], 1)
            v.tensor_single_scalar(woct[0:64, :], woctB0[0:64, :],
                                   P64["Cp"][:, 0:1], ALU.mult)
            D_().then_inc(S["woctr"], 1)
            for j in range(NWT):
                _r, _si = W_SLOT[j]
                v.wait_ge(S[WSEM[_r]], 16 * (_si + 1))
                v.tensor_reduce(wabs[:, j:j + 1], WSTG[_r][_si % 2], AX.X, ALU.add,
                                apply_absolute_value=True)
                D_().then_inc(S["wa"], 1)
                if j == W1_TILES - 1:
                    v.tensor_reduce(TL["wcol"][:, 0:1], wabs[:, 0:W1_TILES],
                                    AX.X, ALU.add)
                    D_().then_inc(S["wcT1"], 1)
                    v.wait_ge(S["wcT1"], 2)
                    v.tensor_reduce(wtot[0:1, 0:1], pmisc[0:1, 128:256], AX.X,
                                    ALU.add); D_()
                    v.tensor_scalar(pack[0:1, 2:3], wtot[0:1, 0:1],
                                    1.0 / (H * D), 1e-5, op0=ALU.mult,
                                    op1=ALU.max); D_()
                    v.reciprocal(pack[0:1, 0:1], pack[0:1, 2:3]); D_()
                    v.tensor_scalar(pack[0:1, 1:2], pack[0:1, 2:3],
                                    1.0 / 127.0, None, op0=ALU.mult)
                    D_().then_inc(S["sc1m"], 1)
            v.wait_ge(S["s1b"], 1)
            v.tensor_reduce(TL["wcol"][:, 0:1], wabs[:, W1_TILES:NWT],
                            AX.X, ALU.add)
            D_().then_inc(S["wcT2"], 1)
            v.wait_ge(S["wcT2"], 2)
            v.tensor_reduce(wtot[0:1, 0:1], pmisc[0:1, 128:256], AX.X,
                            ALU.add); D_()
            v.tensor_scalar(pack[0:1, 2:3], wtot[0:1, 0:1], 1.0 / (H * D), 1e-5,
                            op0=ALU.mult, op1=ALU.max); D_()
            v.reciprocal(pack[0:1, 0:1], pack[0:1, 2:3]); D_()
            v.tensor_scalar(pack[0:1, 1:2], pack[0:1, 2:3], 1.0 / 127.0, None,
                            op0=ALU.mult)
            D_().then_inc(S["sc2m"], 1)
            v.wait_ge(S["s1b"], 1)
            v.wait_ge(S["wotTd"], 1)
            for j in range(NWT):
                rb = TL["r1b"] if j < W1_TILES else TL["r2b"]
                if j == W1_TILES:
                    v.wait_ge(S["s2b"], 1)
                _r, _si = W_SLOT[NWT + j]
                v.wait_ge(S[WSEM[_r]], 16 * (_si + 1))
                if j >= 2:
                    v.wait_ge(S["q1T"], j - 1)
                v.tensor_single_scalar(wmul[:], WSTG[_r][_si % 2], rb[:, 0:1],
                                       ALU.mult); D_()
                v.tensor_scalar(wrnd[:], wmul[:], TL["C_t"][:, 0:1],
                                TL["nC_t"][:, 0:1], op0=ALU.add, op1=ALU.add); D_()
                v.tensor_scalar(wqb[j % 2], wrnd[:], 1.0, -1.0,
                                op0=ALU.min, op1=ALU.max)
                D_().then_inc(S["q1"], 1)

            def ve_scan(it):
                if it == 0:
                    ini = zero64[:, 0:1]
                elif it == NCH:
                    v.tensor_tensor(cgt[:], h_s[(it - 1) % 2][:, 127:128],
                                    P64["gate64"][:], ALU.mult); D_()
                    ini = cgt[:, 0:1]
                else:
                    ini = h_s[(it - 1) % 2][:, 127:128]
                if it - NCH - 1 >= 1:
                    v.wait_ge(S["hb"], it - NCH - 1)
                v.tensor_tensor_scan(h_s[it % 2][:], abt[:], u_s[it % 2][:], ini,
                                     op0=ALU.mult, op1=ALU.add)
                D_().then_inc(S["scan"], 1)

            def ve_out(oi):
                v.wait_ge(S["m2h"], 4 * (oi + 1))
                if oi >= 1:
                    v.wait_ge(S["od"], 16 * oi)
                a2d = TL["a2t"] if oi % 2 == 0 else TL["a2u"]
                for half in range(2):
                    v.scalar_tensor_tensor(out_s[oi % 2][:, half * 512:(half + 1) * 512],
                                           pout[:, half * 512:(half + 1) * 512],
                                           a2d[:, 0:1],
                                           x2[oi % 4][:, half * 512:(half + 1) * 512],
                                           op0=ALU.mult, op1=ALU.add)
                D_().then_inc(S["outv"], 1)

            def ve_fb1a(it):
                oi = it - NCH
                v.wait_ge(S["wout"], oi + 1)
                v.tensor_tensor(x2[oi % 4][:, 0:512], xb[it % 2][:, 0:512],
                                pssm[:, 0:512], ALU.add)
                v.tensor_tensor(x2[oi % 4][:, 512:1024], xb[it % 2][:, 512:1024],
                                pssm[:, 512:1024], ALU.add)
                D_().then_inc(S["x2_"], 1)
                v.tensor_scalar(vs2[:, 0:1024], x2[oi % 4][:], 0.0, None,
                                op0=ALU.add, op1=ALU.add,
                                accum_out=TL["sm2"][:, 0:1]); D_()

            def ve_fb1b(it):
                oi = it - NCH
                v.wait_ge(S["r2d"], oi + 1)
                v.tensor_tensor(TL["va"][:], TL["sm2"][:], TL["sm2"][:],
                                ALU.mult); D_()
                v.scalar_tensor_tensor(TL["vb"][:], TL["va"][:], -INV_D,
                                       TL["sq2u"][:], op0=ALU.mult, op1=ALU.add); D_()
                v.tensor_scalar(TL["vc2"][:], TL["vb"][:], INV_D, 1e-5,
                                op0=ALU.mult, op1=ALU.add)
                D_().then_inc(S["sqr2q"], 1)

            def ve_fa1a(it):
                v.wait_ge(S["lx"], 16 * (it + 1))
                v.tensor_scalar(vs2[:, 0:1024], xb[it % 2][:], 0.0, None,
                                op0=ALU.add, op1=ALU.add,
                                accum_out=TL["sm"][:, 0:1]); D_()

            def ve_fa1b(it):
                v.wait_ge(S["r1d"], it + 1)
                v.tensor_tensor(TL["va"][:], TL["sm"][:], TL["sm"][:], ALU.mult); D_()
                v.scalar_tensor_tensor(TL["vb"][:], TL["va"][:], -INV_D,
                                       TL["sq"][:], op0=ALU.mult, op1=ALU.add); D_()
                v.tensor_scalar(TL["vc"][:], TL["vb"][:], INV_D, 1e-5,
                                op0=ALU.mult, op1=ALU.add)
                D_().then_inc(S["sqr1q"], 1)

            def ve_fa2(it):
                v.wait_ge(S["sqr1a"], it + 1)
                if it >= 1:
                    v.wait_ge(S["xnT"], it)
                v.reciprocal(TL["inv"][:], TL["sqr"][:]); D_()
                v.tensor_tensor(TL["vb"][:], TL["sm"][:], TL["inv"][:], ALU.mult); D_()
                v.tensor_scalar(TL["nb"][:], TL["vb"][:], -INV_D, None,
                                op0=ALU.mult); D_()
                v.tensor_scalar(xn[:], xb[it % 2][:], TL["inv"][:, 0:1],
                                TL["nb"][:, 0:1], op0=ALU.mult, op1=ALU.add)
                D_().then_inc(S["xn_"], 1)

            def ve_fb2(it):
                oi = it - NCH
                v.wait_ge(S["sqr2a"], oi + 1)
                v.reciprocal(TL["inv"][:], TL["sqr2"][:]); D_()
                v.tensor_tensor(TL["vb"][:], TL["sm2"][:], TL["inv"][:],
                                ALU.mult); D_()
                v.tensor_scalar(TL["nb"][:], TL["vb"][:], -INV_D, None,
                                op0=ALU.mult); D_()
                if it + 1 < NIT:
                    v.wait_ge(S["sq1a"], it + 2)
                v.tensor_scalar(xn2[:], x2[oi % 4][:], TL["inv"][:, 0:1],
                                TL["nb"][:, 0:1], op0=ALU.mult, op1=ALU.add); D_()
                v.tensor_reduce(TL["am"][:, 0:1], xn2[:], AX.X, ALU.max,
                                apply_absolute_value=True); D_()
                v.tensor_scalar(TL["cl"][:], TL["am"][:], 1e-5, None,
                                op0=ALU.max); D_()
                v.reciprocal(TL["rr"][:], TL["cl"][:]); D_()
                v.tensor_scalar(TL["sc1"][:], TL["rr"][:], 127.0, None,
                                op0=ALU.mult); D_()
                v.tensor_single_scalar(TL["a1t"][:], TL["cl"][:],
                                       TL["s1o"][:, 0:1], ALU.mult); D_()
                v.tensor_scalar(vs2[:, 0:1024], xn2[:], TL["sc1"][:, 0:1],
                                TL["C_t"][:, 0:1], op0=ALU.mult, op1=ALU.add); D_()
                if oi >= 1:
                    v.wait_ge(S["k1T"], oi)
                v.tensor_scalar(k1s[oi % 2][:], vs2[:, 0:1024],
                                TL["nC_t"][:, 0:1], TL["a1t"][:, 0:1],
                                op0=ALU.add, op1=ALU.mult)
                D_().then_inc(S["k1s_"], 1)

            def ve_uev(it):
                v.wait_ge(S["umm"], it + 1)
                v.tensor_single_scalar(u_s[it % 2][:], pmisc[0:64, 0:128],
                                       P64["Bbar"][:, 0:1], ALU.mult)
                D_().then_inc(S["uev"], 1)

            def ve_back_a(oi, b):
                v.wait_ge(S["gelu"], 8 * (oi + 1))
                h3 = h_sb.rearrange("p (m t) -> p t m", m=32)
                v.tensor_reduce(hred[:, 0:128], h3, AX.X, ALU.max,
                                apply_absolute_value=True)
                D_().then_inc(S["hmax"], 1)
                v.wait_ge(S["hmaxT"], oi + 1)
                v.tensor_reduce(TL["am2"][:, 0:1], pmisc[:, 128:256], AX.X,
                                ALU.max); D_()
                v.tensor_scalar(TL["cl2"][:], TL["am2"][:], 1e-5, None,
                                op0=ALU.max); D_()
                v.reciprocal(TL["r2"][:], TL["cl2"][:]); D_()
                v.tensor_scalar(TL["sc2"][:], TL["r2"][:], 127.0, None,
                                op0=ALU.mult); D_()
                a2d = TL["a2t"] if oi % 2 == 0 else TL["a2u"]
                v.tensor_single_scalar(a2d[:], TL["cl2"][:],
                                       TL["s2o"][:, 0:1], ALU.mult)
                D_().then_inc(S["sc2_"], 1)

            ve_fa1a(0)
            ve_fa1b(0)
            ve_fa2(0)
            ve_fa1a(1)
            ve_fa1b(1)
            ve_fa2(1)
            ve_uev(0)
            ve_scan(0)
            for b in range(NIT + 3):
                oi1 = b - NCH - 1
                oi3 = b - NCH - 3
                if b + 1 < NIT:
                    ve_uev(b + 1)
                    ve_scan(b + 1)
                if 0 <= oi3 < NCH:
                    ve_out(oi3)
                if NCH <= b + 1 < NIT:
                    ve_fb1a(b + 1)
                    ve_fb1b(b + 1)
                if b + 2 < NIT:
                    ve_fa1a(b + 2)
                    ve_fa1b(b + 2)
                if NCH <= b + 1 < NIT:
                    ve_fb2(b + 1)
                if b + 2 < NIT:
                    ve_fa2(b + 2)
                if 0 <= oi1 < NCH:
                    ve_back_a(oi1, b)

        @block.tensor
        def _(t):
            t.wait_ge(S["prep"], 1)
            t.wait_ge(S["k1m"], 1)
            t.matmul(pmisc[:, 0:2], ones1f[:], pack[0:1, 0:2], start=True,
                     stop=True).then_inc(S["k1m"], 1)
            t.wait_ge(S["winc"], 1)
            for a in range(8):
                ti = t.transpose(pT[:, a * 64:(a + 1) * 64],
                                 winB[0:64, a * 128:(a + 1) * 128],
                                 identb[0:64, 0:64])
            ti.then_inc(S["winTd"], 1)
            t.wait_ge(S["wotc"], 1)
            t.wait_ge(S["winTe"], 1)
            for a in range(8):
                ti = t.transpose(pT[0:64, a * 128:(a + 1) * 128],
                                 wqb[0][:, a * 64:(a + 1) * 64], identb[:])
            ti.then_inc(S["wotTd"], 1)
            t.wait_ge(S["wcT1"], 1)
            t.transpose(pmisc[0:1, 128:256], TL["wcol"][:, 0:1],
                        ident32[:]).then_inc(S["wcT1"], 1)
            t.wait_ge(S["sc1m"], 1)
            t.matmul(pmisc[:, 2:4], ones1f[:], pack[0:1, 0:2], start=True,
                     stop=True).then_inc(S["sc1m"], 1)
            t.wait_ge(S["wcT2"], 1)
            t.transpose(pmisc[0:1, 128:256], TL["wcol"][:, 0:1],
                        ident32[:]).then_inc(S["wcT2"], 1)
            t.wait_ge(S["sc2m"], 1)
            t.matmul(pmisc[:, 4:6], ones1f[:], pack[0:1, 0:2], start=True,
                     stop=True).then_inc(S["sc2m"], 1)
            t.wait_ge(S["wotTe"], 1)
            for j in range(NWT):
                t.wait_ge(S["q1"], j + 1)
                if j >= 1:
                    t.wait_ge(S["q1e"], j)
                for f in range(8):
                    ti = t.transpose(pT[:, f * 128:(f + 1) * 128],
                                     wqb[j % 2][:, f * 128:(f + 1) * 128],
                                     identb[:])
                ti.then_inc(S["q1T"], 1)

            def te_k1sT(oi):
                t.wait_ge(S["k1s_"], oi + 1)
                t.wait_ge(S["pTe"], PT_IDX[("k", oi)])
                for f in range(8):
                    ti = t.transpose(pT[:, f * 128:(f + 1) * 128],
                                     k1s[oi % 2][:, f * 128:(f + 1) * 128],
                                     identb[:])
                ti.then_inc(S["k1T"], 1)

            def te_wout(it):
                oi = it - NCH
                t.wait_ge(S["hb"], oi + 1)
                if oi == 0:
                    t.wait_ge(S["woctr"], 1)
                if oi >= 1:
                    t.wait_ge(S["x2_"], oi)
                t.matmul(pssm[:, 0:512], hb_s[:], woct[:, 0:512], start=True,
                         stop=True)
                t.matmul(pssm[:, 512:1024], hb_s[:], woct[:, 512:1024],
                         start=True, stop=True).then_inc(S["wout"], 1)

            def te_mm1(oi, wout_it=None, k1sT_oi=None):
                t.wait_ge(S["k1fm_"], oi + 1)
                if oi == 0:
                    t.wait_ge(S["q1e"], W1_TILES)
                for g in range(8):
                    G = 8 * oi + g
                    bank = pm1a if g % 2 == 0 else pm1b
                    if G >= 2:
                        t.wait_ge(S["gelu"], G - 1)
                    for mi in range(4):
                        m = g * 4 + mi
                        reg = bank[:, mi * 128:(mi + 1) * 128]
                        for k in range(8):
                            ti = t.matmul(
                                reg,
                                t1T[:, k * 4096 + m * 128:k * 4096 + (m + 1) * 128],
                                k1fm[oi % 2][:, k * 128:(k + 1) * 128],
                                start=(k == 0), stop=(k == 7))
                    ti.then_inc(S["mm1"], 1)
                    if g == 0:
                        if wout_it is not None:
                            te_wout(wout_it)
                        if k1sT_oi is not None:
                            te_k1sT(k1sT_oi)

            def te_fa(it):
                t.wait_ge(S["xn_"], it + 1)
                pt_use = PT_IDX[("x", it)]
                if it == 0:
                    t.wait_ge(S["q1e"], NWT)
                else:
                    t.wait_ge(S["pTe"], pt_use)
                for f in range(8):
                    ti = t.transpose(pT[:, f * 128:(f + 1) * 128],
                                     xn[:, f * 128:(f + 1) * 128], identb[:])
                ti.then_inc(S["xnT"], 1)
                t.wait_ge(S["xnfm_"], it + 1)
                if it == 0:
                    t.wait_ge(S["winTe"], 1)
                t.wait_ge(S["uev"], it)
                if it >= NCH + 3:
                    t.wait_ge(S["bce"], it - NCH - 2)
                for k in range(8):
                    ti = t.matmul(pmisc[0:64, 0:128], winT[:, k * 64:(k + 1) * 64],
                                  xnfm[it % 2][:, k * 128:(k + 1) * 128],
                                  start=(k == 0), stop=(k == 7))
                ti.then_inc(S["umm"], 1)

            def te_btr(oi, b):
                t.wait_ge(S["hmax"], oi + 1)
                t.wait_ge(S["uev"], min(b + 2, NIT))
                t.transpose(pmisc[:, 128:256], hred[:],
                            ident32[:]).then_inc(S["hmaxT"], 1)
                t.wait_ge(S["sc2_"], oi + 1)
                t.transpose(pmisc[0:1, 256:384], TL["sc2"][:, 0:1],
                            ident32[:]).then_inc(S["rowT"], 1)
                t.wait_ge(S["rowe"], oi + 1)
                t.matmul(pmisc[:, 384:512], ones1f[:], rowf[:], start=True,
                         stop=True).then_inc(S["bcmm"], 1)

            def te_mm2(oi, tiles):
                if tiles[0] == 0:
                    t.wait_ge(S["k2q"], 4 * oi + 2)
                    if oi == 0:
                        t.wait_ge(S["q1e"], NWT)
                    if oi >= 1:
                        t.wait_ge(S["outv"], oi)
                else:
                    t.wait_ge(S["k2q"], 4 * oi + 4)
                for hk in tiles:
                    for d in range(2):
                        ti = t.matmul(
                            pout[:, d * 512:(d + 1) * 512],
                            k2[:, hk * 128:(hk + 1) * 128],
                            t2T[:, hk * 1024 + d * 512:hk * 1024 + d * 512 + 512],
                            start=(hk == 0), stop=(hk == 31))
                    if hk % 8 == 7:
                        ti.then_inc(S["m2h"], 1)

            te_fa(0)
            for b in range(NIT + 2):
                oi1 = b - NCH - 1
                oi2 = b - NCH - 2
                has1 = 0 <= oi1 < NCH
                has2 = 0 <= oi2 < NCH
                w_it = b + 1 if NCH <= b + 1 < NIT else None
                k_oi = b - NCH if 0 <= b - NCH < NCH else None
                if b + 1 < NIT:
                    te_fa(b + 1)
                if has1:
                    te_mm1(oi1, wout_it=w_it, k1sT_oi=k_oi)
                else:
                    if w_it is not None:
                        te_wout(w_it)
                    if k_oi is not None:
                        te_k1sT(k_oi)
                if has2:
                    te_mm2(oi2, range(0, 16))
                if has1:
                    te_btr(oi1, b)
                if has2:
                    te_mm2(oi2, range(16, 32))

    return nc


def kernel(**inputs):
    x = np.ascontiguousarray(np.asarray(inputs["L0_attn_x"], np.float32))
    xs = x.reshape(NCORE, TOK, D)
    w1 = np.ascontiguousarray(np.asarray(inputs["L0_ffn_w1"], np.float32))
    w2 = np.ascontiguousarray(np.asarray(inputs["L0_ffn_w2"], np.float32))
    win = np.ascontiguousarray(np.asarray(inputs["L0_attn_W_in"], np.float32))
    wout = np.ascontiguousarray(np.asarray(inputs["L0_attn_W_out"], np.float32))
    logA = np.asarray(inputs["L0_attn_log_A"], np.float32).reshape(N, 1)
    Bp = np.asarray(inputs["L0_attn_B_param"], np.float32).reshape(N, 1)
    Cp = np.asarray(inputs["L0_attn_C_param"], np.float32).reshape(N, 1)
    logdt = np.asarray(inputs["L0_attn_log_dt"], np.float32).reshape(1, 1)

    nc = build_nc()
    in_maps = []
    for i in range(NCORE):
        odd = i % 2 == 1
        in_maps.append({
            "xh": xs[i - 1] if odd else xs[i],
            "xo": xs[i],
            "w1": w1, "w2": w2, "win": win, "wout": wout,
            "logA": logA, "Bp": Bp, "Cp": Cp, "logdt": logdt,
            "gate": np.array([[1.0 if odd else 0.0]], np.float32),
        })
    res = run_bass_kernel_spmd(nc, in_maps, core_ids=list(range(NCORE)))
    out = np.concatenate([res.results[i]["out"] for i in range(NCORE)], axis=0)
    return out.reshape(B, L, D)


# version 15
# speedup vs baseline: 1.4287x; 1.4287x over previous
import numpy as np
from contextlib import ExitStack
import concourse.bass as bass
import concourse.mybir as mybir
from concourse.bass_utils import run_bass_kernel_spmd

F32 = mybir.dt.float32
BF16 = mybir.dt.bfloat16
AF = mybir.ActivationFunctionType
ALU = mybir.AluOpType
AX = mybir.AxisListType
C32 = float(1.5 * 2.0**23)
C64 = float(1.5 * 2.0**52)

B, L, D, N, H = 4, 8192, 1024, 64, 4096
NCORE = 8
TOK = L // 2
NCH = TOK // 128
NIT = 2 * NCH
INV_D = 1.0 / D
W1_TILES = 32
W2_TILES = 32
NWT = W1_TILES + W2_TILES

_PT_ORDER = [("x", 0)]
for _b in range(NIT + 2):
    if _b + 1 < NIT:
        _PT_ORDER.append(("x", _b + 1))
    if 0 <= _b - NCH < NCH:
        _PT_ORDER.append(("k", _b - NCH))
PT_IDX = {key: i for i, key in enumerate(_PT_ORDER)}

W_SLOT = []
_rc = [0, 0, 0]
for _t in range(2 * NWT):
    _r = (_t % NWT) % 3
    W_SLOT.append((_r, _rc[_r]))
    _rc[_r] += 1
W_RING = [[t for t in range(2 * NWT) if W_SLOT[t][0] == r] for r in range(3)]


def build_nc():
    nc = bass.Bass("TRN2", target_bir_lowering=False)
    dr = {}
    for n, shp in [("xh", (TOK, D)), ("xo", (TOK, D)),
                   ("w1", (H, D)), ("w2", (D, H)),
                   ("win", (N, D)), ("wout", (D, N)),
                   ("logA", (N, 1)), ("Bp", (N, 1)), ("Cp", (N, 1)),
                   ("logdt", (1, 1)), ("gate", (1, 1))]:
        dr[n] = nc.dram_tensor(n, list(shp), F32, kind="ExternalInput").ap()
    dr["out"] = nc.dram_tensor("out", [TOK, D], F32, kind="ExternalOutput").ap()

    sbt = lambda n, shp, dt=F32: nc.alloc_sbuf_tensor(n, list(shp), dt).ap()
    t1T = sbt("t1T", (128, 8 * H), BF16)
    t2T = sbt("t2T", (128, 32 * D), BF16)
    winT = sbt("winT", (128, 8 * N), BF16)
    woct = sbt("woct", (64, D), BF16)
    xb = [sbt(f"xb{i}", (128, D)) for i in range(2)]
    xn = sbt("xn", (128, D), BF16)
    xnfm = [sbt(f"xnfm{i}", (128, D), BF16) for i in range(2)]
    x2_3 = [sbt(f"x2_{i}", (128, D)) for i in range(3)]
    x2 = None
    xn2 = sbt("xn2", (128, D))
    k1s0 = sbt("k1s0", (128, D), BF16)
    k1s = [k1s0, k1s0]
    k1fm0 = sbt("k1fm0", (128, D), BF16)
    k1fm = [k1fm0, k1fm0]
    h_sb = sbt("h_sb", (128, H))
    vs2 = sbt("vs2", (128, 2048))
    x2 = x2_3 + [vs2[:, 1024:2048]]
    k2 = sbt("k2", (128, H), BF16)
    sc2bc = sbt("sc2bc", (128, 128))
    hred = sbt("hred", (128, 128))
    rowf = sbt("rowf", (1, 128))
    out_s0 = sbt("out_s0", (128, D))
    out_s = [out_s0, out_s0]
    u_s = [sbt(f"u_s{i}", (64, 128)) for i in range(2)]
    h_s = [sbt(f"h_s{i}", (64, 128)) for i in range(2)]
    hb_s = sbt("hb_s", (64, 128), BF16)
    abt = sbt("abt", (64, 128))
    cgt = sbt("cgt", (64, 1))
    zero64 = sbt("zero64", (64, 1))
    identb = sbt("identb", (128, 128), BF16)
    ident32 = sbt("ident32", (128, 128))
    ones1f = sbt("ones1f", (1, 128))
    TLn = ["sm", "sq", "va", "vb", "vc", "sqr", "inv", "nb", "am", "cl", "rr",
           "sc1", "a1t", "am2", "cl2", "r2", "sc2", "a2t", "C_t", "nC_t",
           "one_t", "chi", "c64", "nc64", "r1b", "s1o", "r2b", "s2o", "wcol",
           "vc2", "sqr2", "a2u", "sm2", "sq2u"]
    TLblk = sbt("TLblk", (128, len(TLn)), F32)
    TL = {n: TLblk[:, i:i + 1] for i, n in enumerate(TLn)}
    P64n = ["logA", "Bp", "Cp", "expA", "Aneg", "dtb", "gate64", "lnab",
            "Abar", "am1", "mask", "d1", "e1", "den", "rden", "f0", "f1",
            "f2", "f3", "frac", "Bbar"]
    P64blk = sbt("P64blk", (64, len(P64n)), F32)
    P64 = {n: P64blk[:, i:i + 1] for i, n in enumerate(P64n)}
    sc11 = sbt("sc11", (1, 1))
    sc12 = sbt("sc12", (1, 1))
    gt11 = sbt("gt11", (1, 1))
    pack = sbt("pack", (1, 4))
    wtot = sbt("wtot", (1, 1))
    onesn = sc2bc[0:64, 0:128]
    identD = hred
    k1 = vs2.bitcast(BF16)[:, 2048:3072]
    scr_sq = xn2
    scr_vt = vs2[:, 0:1024]
    scr_sq = xn2
    wst = [vs2[:, 0:1024], vs2[:, 1024:2048]]
    wstB = [h_sb[:, 1024:2048], h_sb[:, 2048:3072]]
    wstC = [h_sb[:, 3072:4096], out_s0[:, 0:1024]]
    WSTG = [wst, wstB, wstC]
    wqb = [k2[:, 0:1024], k2[:, 1024:2048]]
    winB = k2[:, 2048:3072]
    woctB0 = k2[:, 3072:4096]
    wost = h_sb[:, 0:512]
    winF = xn2
    wmul = x2_3[0]
    wrnd = x2_3[1]
    wabs = hred
    pT = nc.alloc_psum_tensor("pT", [128, 1024], BF16).ap()
    pmisc = nc.alloc_psum_tensor("pmisc", [128, 512], F32).ap()
    pssm = nc.alloc_psum_tensor("pssm", [128, 1024], F32).ap()
    pm1a = nc.alloc_psum_tensor("pm1a", [128, 512], F32).ap()
    pm1b = nc.alloc_psum_tensor("pm1b", [128, 512], F32).ap()
    pout = nc.alloc_psum_tensor("pout", [128, 1024], F32).ap()

    t1T_kv = t1T.rearrange("p (k r) -> p k r", k=8)
    t2T_qv = t2T.rearrange("p (q f r) -> p q f r", q=4, f=8)
    pT_v8 = pT.rearrange("p (k t) -> p k t", k=8)

    SEMS = ["lx", "lw", "lws", "lwg", "lp", "lo", "gp", "prep",
            "wa", "s1b", "s2b", "q1", "q1T", "q1e",
            "zoh1", "zoh2", "k1m", "k1me", "winc", "winTd", "winTe",
            "wotc", "wotTd", "wotTe", "woctr", "wcT1", "wcT2", "sc1m", "sc2m",
            "sq1a", "sq2a", "r1d", "r2d", "sqr1q", "sqr1a", "xn_", "xnT", "xnfm_", "umm", "scan",
            "hb", "wout", "x2_", "sqr2q", "sqr2a", "uev",
            "k1s_", "k1T", "k1fm_", "mm1", "gelu", "hmax", "hmaxT", "sc2_",
            "rowT", "rowe", "bcmm", "bce", "k2_", "m2h", "outv", "od", "pTe"]

    with ExitStack() as ctx:
        S = {n: ctx.enter_context(nc.semaphore(n)) for n in SEMS}
        block = ctx.enter_context(nc.Block())

        WSEM = ["lw", "lws", "lwg"]
        w1_v = dr["w1"].rearrange("(j p) d -> j p d", p=128)
        w2_v = dr["w2"].rearrange("(i p) (q d) -> i q p d", p=128, d=1024)

        def wtile(j):
            return w1_v[j] if j < W1_TILES else \
                w2_v[(j - W1_TILES) // 4, (j - W1_TILES) % 4]

        def w_dma(eng, r, t):
            _, si = W_SLOT[t]
            if si >= 2:
                tp = W_RING[r][si - 2]
                if tp < NWT:
                    eng.wait_ge(S["wa"], tp + 1)
                else:
                    eng.wait_ge(S["q1"], tp - NWT + 1)
            eng.wait_ge(S[WSEM[r]], 16 * si)
            eng.dma_start(out=WSTG[r][si % 2],
                          in_=wtile(t % NWT)).then_inc(S[WSEM[r]], 16)

        @block.sync
        def _(sync):
            for di, nm in enumerate(["logA", "Bp", "Cp", "logdt", "gate", "win"]):
                dst = {"logA": P64["logA"], "Bp": P64["Bp"], "Cp": P64["Cp"],
                       "logdt": sc11, "gate": gt11}.get(nm)
                d = winF[0:64, :] if nm == "win" else dst[:]
                sync.wait_ge(S["lp"], 16 * di)
                sync.dma_start(out=d, in_=dr[nm][:, :]).then_inc(S["lp"], 16)
            wo_v = dr["wout"].rearrange("(a p) n -> a p n", p=128)
            for a in range(8):
                sync.wait_ge(S["lo"], 16 * a)
                sync.dma_start(out=wost[:, a * 64:(a + 1) * 64],
                               in_=wo_v[a]).then_inc(S["lo"], 16)
            for t in W_RING[1]:
                w_dma(sync, 1, t)
            for it in range(NIT):
                sync.wait_ge(S["lx"], 16 * it)
                if it >= 2:
                    sync.wait_ge(S["xn_"], it - 1)
                    sync.wait_ge(S["sq1a"], it - 1)
                    if it - 2 >= NCH:
                        sync.wait_ge(S["x2_"], it - 2 - NCH + 1)
                    else:
                        sync.wait_ge(S["scan"], it - 1)
                src = dr["xh"] if it < NCH else dr["xo"]
                c = it if it < NCH else it - NCH
                sync.dma_start(out=xb[it % 2][:],
                               in_=src[c * 128:(c + 1) * 128, :]).then_inc(S["lx"], 16)

        @block.gpsimd
        def _(g):
            g.iota(identD[:], pattern=[[-1, 128]], base=0, channel_multiplier=1,
                   allow_small_or_imprecise_dtypes=True)
            g.drain().then_inc(S["gp"], 1)
            for t in W_RING[2]:
                w_dma(g, 2, t)
            def gp_sq2(it):
                oi = it - NCH
                g.wait_ge(S["x2_"], oi + 1)
                if it >= NCH + 1:
                    g.wait_ge(S["k1s_"], it - NCH)
                g.tensor_tensor(scr_sq[:], x2[oi % 4][:], x2[oi % 4][:],
                                ALU.mult)
                g.drain().then_inc(S["sq2a"], 1)

            def gp_sq1(it):
                g.wait_ge(S["lx"], 16 * (it + 1))
                if it >= 1:
                    g.wait_ge(S["r1d"], it)
                if it - NCH >= 1:
                    g.wait_ge(S["r2d"], it - NCH)
                g.tensor_tensor(scr_sq[:], xb[it % 2][:], xb[it % 2][:],
                                ALU.mult)
                g.drain().then_inc(S["sq1a"], 1)

            gp_sq1(0)
            gp_sq1(1)
            for b in range(NIT + 3):
                oi3 = b - NCH - 3
                if NCH <= b + 1 < NIT:
                    gp_sq2(b + 1)
                if b + 2 < NIT:
                    gp_sq1(b + 2)
                if 0 <= oi3 < NCH:
                    g.wait_ge(S["od"], 16 * oi3)
                    g.wait_ge(S["outv"], oi3 + 1)
                    g.dma_start(out=dr["out"][oi3 * 128:(oi3 + 1) * 128, :],
                                in_=out_s[oi3 % 2][:]).then_inc(S["od"], 16)
            g.wait_ge(S["od"], 16 * NCH)

        @block.scalar
        def _(s):
            s.wait_ge(S["lp"], 16 * 6)
            s.activation(P64["expA"][:], P64["logA"][:], AF.Exp, bias=0.0, scale=1.0)
            s.activation(sc12[:], sc11[:], AF.Exp, bias=0.0, scale=1.0)
            s.drain().then_inc(S["zoh1"], 1)
            s.wait_ge(S["k1m"], 2)
            s.copy(P64["dtb"][:], pmisc[0:64, 0:1])
            s.copy(P64["gate64"][:], pmisc[0:64, 1:2])
            s.drain().then_inc(S["k1me"], 1)
            s.wait_ge(S["zoh2"], 1)
            s.activation(P64["Abar"][:], P64["lnab"][:], AF.Exp, bias=0.0, scale=1.0)
            s.drain().then_inc(S["zoh2"], 1)
            s.wait_ge(S["winTd"], 1)
            s.copy(winT[:], pT[:, 0:512])
            s.drain().then_inc(S["winTe"], 1)
            s.wait_ge(S["wotTd"], 1)
            s.copy(woctB0[0:64, :], pT[0:64, 0:1024])
            s.drain().then_inc(S["wotTe"], 1)
            for t in W_RING[0]:
                if t < NWT:
                    w_dma(s, 0, t)
            s.wait_ge(S["sc1m"], 2)
            s.copy(TL["r1b"][:], pmisc[:, 2:3])
            s.copy(TL["s1o"][:], pmisc[:, 3:4])
            s.drain().then_inc(S["s1b"], 1)

            def evac_ternary(jm):
                s.wait_ge(S["q1T"], jm + 1)
                if jm < W1_TILES:
                    s.copy(t1T_kv[:, :, jm * 128:(jm + 1) * 128], pT_v8[:])
                else:
                    ii, qq = (jm - W1_TILES) // 4, (jm - W1_TILES) % 4
                    s.copy(t2T_qv[:, qq:qq + 1, :, ii * 128:(ii + 1) * 128],
                           pT_v8.unsqueeze(1))
                s.drain().then_inc(S["q1e"], 1)

            for j in range(NWT):
                if j == W1_TILES:
                    s.wait_ge(S["sc2m"], 2)
                    s.copy(TL["r2b"][:], pmisc[:, 4:5])
                    s.copy(TL["s2o"][:], pmisc[:, 5:6])
                    s.drain().then_inc(S["s2b"], 1)
                if j % 3 == 0:
                    w_dma(s, 0, NWT + j)
                if j >= 1:
                    evac_ternary(j - 1)
            evac_ternary(NWT - 1)

            def se_hb(it):
                s.wait_ge(S["scan"], it + 1)
                s.copy(hb_s[:], h_s[it % 2][:])
                s.drain().then_inc(S["hb"], 1)

            def se_k1fm(oi):
                s.wait_ge(S["k1T"], oi + 1)
                if oi >= 1:
                    s.wait_ge(S["mm1"], 8 * oi)
                s.copy(k1fm[oi % 2][:], pT[:])
                s.drain().then_inc(S["k1fm_"], 1)
                s.drain().then_inc(S["pTe"], 1)

            def se_gelu(oi, g):
                s.wait_ge(S["mm1"], 8 * oi + g + 1)
                if g == 0 and oi >= 1:
                    s.wait_ge(S["k2_"], oi)
                if oi == 0 and g == 2:
                    s.wait_ge(S["q1"], NWT)
                bank = pm1a if g % 2 == 0 else pm1b
                s.activation(h_sb[:, g * 512:(g + 1) * 512], bank[:], AF.Gelu,
                             bias=0.0, scale=1.0)
                s.drain().then_inc(S["gelu"], 1)

            def se_copy2(it):
                oi = it - NCH
                s.wait_ge(S["sq2a"], oi + 1)
                s.wait_ge(S["xnT"], it + 1)
                s.activation(xn[:], scr_sq[:], AF.Copy, bias=0.0, scale=1.0,
                             accum_out=TL["sq2u"][:, 0:1])
                s.drain().then_inc(S["r2d"], 1)

            def se_copy1(it):
                s.wait_ge(S["sq1a"], it + 1)
                if it >= 1:
                    s.wait_ge(S["xnT"], it)
                s.activation(xn[:], scr_sq[:], AF.Copy, bias=0.0, scale=1.0,
                             accum_out=TL["sq"][:, 0:1])
                s.drain().then_inc(S["r1d"], 1)

            def se_sqrt2(it):
                oi = it - NCH
                s.wait_ge(S["sqr2q"], oi + 1)
                s.activation(TL["sqr2"][:], TL["vc2"][:], AF.Sqrt, bias=0.0,
                             scale=1.0)
                s.drain().then_inc(S["sqr2a"], 1)

            def se_sqrt1(it):
                s.wait_ge(S["sqr1q"], it + 1)
                s.activation(TL["sqr"][:], TL["vc"][:], AF.Sqrt, bias=0.0,
                             scale=1.0)
                s.drain().then_inc(S["sqr1a"], 1)

            def se_xnfm(it):
                s.wait_ge(S["xnT"], it + 1)
                if it >= 2:
                    s.wait_ge(S["umm"], it - 1)
                s.copy(xnfm[it % 2][:], pT[:])
                s.drain().then_inc(S["xnfm_"], 1)
                s.drain().then_inc(S["pTe"], 1)

            def se_rowbc(oi):
                s.wait_ge(S["rowT"], oi + 1)
                s.copy(rowf[:], pmisc[0:1, 256:384])
                s.drain().then_inc(S["rowe"], 1)
                s.wait_ge(S["bcmm"], oi + 1)
                s.copy(sc2bc[:], pmisc[:, 384:512])
                s.drain().then_inc(S["bce"], 1)

            se_copy1(0)
            se_sqrt1(0)
            se_copy1(1)
            se_sqrt1(1)
            se_xnfm(0)
            for b in range(NIT + 2):
                oi1 = b - NCH - 1
                has1 = 0 <= oi1 < NCH
                if b + 1 < NIT:
                    se_xnfm(b + 1)
                if NCH <= b + 1 < NIT:
                    se_hb(b + 1)
                if has1:
                    for g in range(0, 4):
                        se_gelu(oi1, g)
                if NCH <= b + 1 < NIT:
                    se_copy2(b + 1)
                if has1:
                    se_gelu(oi1, 4)
                if b + 2 < NIT:
                    se_copy1(b + 2)
                if has1:
                    se_gelu(oi1, 5)
                if NCH <= b + 1 < NIT:
                    se_sqrt2(b + 1)
                if b + 2 < NIT:
                    se_sqrt1(b + 2)
                if has1:
                    se_gelu(oi1, 6)
                    se_gelu(oi1, 7)
                if 0 <= b - NCH < NCH:
                    se_k1fm(b - NCH)
                if has1:
                    se_rowbc(oi1)

        @block.vector
        def _(v):
            D_ = v.drain
            v.wait_ge(S["gp"], 1)
            v.tensor_single_scalar(identb[:], identD[:], 0.0, ALU.is_equal)
            v.tensor_single_scalar(ident32[:], identD[:], 0.0, ALU.is_equal)
            v.memset(ones1f[:], 1.0)
            v.memset(onesn[:], 1.0)
            v.memset(zero64[:], 0.0)
            v.memset(TL["one_t"][:], 1.0)
            v.memset(TL["c64"][:], C64)
            v.memset(TL["nc64"][:], -C64)
            D_()
            v.tensor_scalar(TL["chi"][:], TL["one_t"][:], TL["c64"][:, 0:1],
                            TL["nc64"][:, 0:1], op0=ALU.add, op1=ALU.add); D_()
            v.tensor_scalar(TL["va"][:], TL["chi"][:], TL["c64"][:, 0:1], C32,
                            op0=ALU.mult, op1=ALU.add); D_()
            v.tensor_copy(TL["C_t"][:], TL["va"][:]); D_()
            v.tensor_scalar(TL["nC_t"][:], TL["va"][:], -1.0, None, op0=ALU.mult)
            D_().then_inc(S["prep"], 1)
            v.wait_ge(S["zoh1"], 1)
            v.tensor_scalar(P64["Aneg"][:], P64["expA"][:], -1.0, None,
                            op0=ALU.mult); D_()
            v.tensor_copy(pack[0:1, 0:1], sc12[:])
            v.tensor_copy(pack[0:1, 1:2], gt11[:])
            D_().then_inc(S["k1m"], 1)
            v.wait_ge(S["k1me"], 1)
            v.tensor_tensor(P64["lnab"][:], P64["Aneg"][:], P64["dtb"][:],
                            ALU.mult)
            D_().then_inc(S["zoh2"], 1)
            v.wait_ge(S["zoh2"], 2)
            v.tensor_scalar(P64["am1"][:], P64["Abar"][:], -1.0, None,
                            op0=ALU.add); D_()
            v.tensor_single_scalar(P64["mask"][:], P64["expA"][:], 1e-6,
                                   ALU.is_lt); D_()
            v.tensor_tensor(P64["d1"][:], P64["mask"][:], P64["Aneg"][:],
                            ALU.mult); D_()
            v.tensor_tensor(P64["e1"][:], P64["Aneg"][:], P64["d1"][:],
                            ALU.subtract); D_()
            v.tensor_tensor(P64["den"][:], P64["e1"][:], P64["mask"][:],
                            ALU.add); D_()
            v.reciprocal(P64["rden"][:], P64["den"][:]); D_()
            v.tensor_tensor(P64["f0"][:], P64["am1"][:], P64["rden"][:],
                            ALU.mult); D_()
            v.tensor_tensor(P64["f1"][:], P64["mask"][:], P64["dtb"][:],
                            ALU.mult); D_()
            v.tensor_tensor(P64["f2"][:], P64["mask"][:], P64["f0"][:],
                            ALU.mult); D_()
            v.tensor_tensor(P64["f3"][:], P64["f0"][:], P64["f2"][:],
                            ALU.subtract); D_()
            v.tensor_tensor(P64["frac"][:], P64["f3"][:], P64["f1"][:],
                            ALU.add); D_()
            v.tensor_tensor(P64["Bbar"][:], P64["frac"][:], P64["Bp"][:],
                            ALU.mult); D_()
            v.tensor_single_scalar(abt[:], onesn[:], P64["Abar"][:, 0:1],
                                   ALU.mult); D_()
            v.wait_ge(S["lp"], 16 * 6)
            v.wait_ge(S["lo"], 16 * 8)
            v.tensor_copy(winB[0:64, :], winF[0:64, :])
            D_().then_inc(S["winc"], 1)
            v.tensor_copy(wqb[0][:, 0:512], wost[:, 0:512])
            D_().then_inc(S["wotc"], 1)
            v.wait_ge(S["wotTe"], 1)
            v.tensor_single_scalar(woct[0:64, :], woctB0[0:64, :],
                                   P64["Cp"][:, 0:1], ALU.mult)
            D_().then_inc(S["woctr"], 1)
            for j in range(NWT):
                _r, _si = W_SLOT[j]
                v.wait_ge(S[WSEM[_r]], 16 * (_si + 1))
                v.tensor_reduce(wabs[:, j:j + 1], WSTG[_r][_si % 2], AX.X, ALU.add,
                                apply_absolute_value=True)
                D_().then_inc(S["wa"], 1)
                if j == W1_TILES - 1:
                    v.tensor_reduce(TL["wcol"][:, 0:1], wabs[:, 0:W1_TILES],
                                    AX.X, ALU.add)
                    D_().then_inc(S["wcT1"], 1)
                    v.wait_ge(S["wcT1"], 2)
                    v.tensor_reduce(wtot[0:1, 0:1], pmisc[0:1, 128:256], AX.X,
                                    ALU.add); D_()
                    v.tensor_scalar(pack[0:1, 2:3], wtot[0:1, 0:1],
                                    1.0 / (H * D), 1e-5, op0=ALU.mult,
                                    op1=ALU.max); D_()
                    v.reciprocal(pack[0:1, 0:1], pack[0:1, 2:3]); D_()
                    v.tensor_scalar(pack[0:1, 1:2], pack[0:1, 2:3],
                                    1.0 / 127.0, None, op0=ALU.mult)
                    D_().then_inc(S["sc1m"], 1)
            v.wait_ge(S["s1b"], 1)
            v.tensor_reduce(TL["wcol"][:, 0:1], wabs[:, W1_TILES:NWT],
                            AX.X, ALU.add)
            D_().then_inc(S["wcT2"], 1)
            v.wait_ge(S["wcT2"], 2)
            v.tensor_reduce(wtot[0:1, 0:1], pmisc[0:1, 128:256], AX.X,
                            ALU.add); D_()
            v.tensor_scalar(pack[0:1, 2:3], wtot[0:1, 0:1], 1.0 / (H * D), 1e-5,
                            op0=ALU.mult, op1=ALU.max); D_()
            v.reciprocal(pack[0:1, 0:1], pack[0:1, 2:3]); D_()
            v.tensor_scalar(pack[0:1, 1:2], pack[0:1, 2:3], 1.0 / 127.0, None,
                            op0=ALU.mult)
            D_().then_inc(S["sc2m"], 1)
            v.wait_ge(S["s1b"], 1)
            v.wait_ge(S["wotTd"], 1)
            for j in range(NWT):
                rb = TL["r1b"] if j < W1_TILES else TL["r2b"]
                if j == W1_TILES:
                    v.wait_ge(S["s2b"], 1)
                _r, _si = W_SLOT[NWT + j]
                v.wait_ge(S[WSEM[_r]], 16 * (_si + 1))
                if j >= 2:
                    v.wait_ge(S["q1T"], j - 1)
                v.tensor_single_scalar(wmul[:], WSTG[_r][_si % 2], rb[:, 0:1],
                                       ALU.mult); D_()
                v.tensor_scalar(wrnd[:], wmul[:], TL["C_t"][:, 0:1],
                                TL["nC_t"][:, 0:1], op0=ALU.add, op1=ALU.add); D_()
                v.tensor_scalar(wqb[j % 2], wrnd[:], 1.0, -1.0,
                                op0=ALU.min, op1=ALU.max)
                D_().then_inc(S["q1"], 1)

            def ve_scan(it):
                if it == 0:
                    ini = zero64[:, 0:1]
                elif it == NCH:
                    v.tensor_tensor(cgt[:], h_s[(it - 1) % 2][:, 127:128],
                                    P64["gate64"][:], ALU.mult); D_()
                    ini = cgt[:, 0:1]
                else:
                    ini = h_s[(it - 1) % 2][:, 127:128]
                if it - NCH - 1 >= 1:
                    v.wait_ge(S["hb"], it - NCH - 1)
                v.tensor_tensor_scan(h_s[it % 2][:], abt[:], u_s[it % 2][:], ini,
                                     op0=ALU.mult, op1=ALU.add)
                D_().then_inc(S["scan"], 1)

            def ve_out(oi):
                v.wait_ge(S["m2h"], 4 * (oi + 1))
                if oi >= 1:
                    v.wait_ge(S["od"], 16 * oi)
                a2d = TL["a2t"] if oi % 2 == 0 else TL["a2u"]
                for half in range(2):
                    v.scalar_tensor_tensor(out_s[oi % 2][:, half * 512:(half + 1) * 512],
                                           pout[:, half * 512:(half + 1) * 512],
                                           a2d[:, 0:1],
                                           x2[oi % 4][:, half * 512:(half + 1) * 512],
                                           op0=ALU.mult, op1=ALU.add)
                D_().then_inc(S["outv"], 1)

            def ve_fb1a(it):
                oi = it - NCH
                v.wait_ge(S["wout"], oi + 1)
                v.tensor_tensor(x2[oi % 4][:, 0:512], xb[it % 2][:, 0:512],
                                pssm[:, 0:512], ALU.add)
                v.tensor_tensor(x2[oi % 4][:, 512:1024], xb[it % 2][:, 512:1024],
                                pssm[:, 512:1024], ALU.add)
                D_().then_inc(S["x2_"], 1)
                v.tensor_scalar(vs2[:, 0:1024], x2[oi % 4][:], 0.0, None,
                                op0=ALU.add, op1=ALU.add,
                                accum_out=TL["sm2"][:, 0:1]); D_()

            def ve_fb1b(it):
                oi = it - NCH
                v.wait_ge(S["r2d"], oi + 1)
                v.tensor_tensor(TL["va"][:], TL["sm2"][:], TL["sm2"][:],
                                ALU.mult); D_()
                v.scalar_tensor_tensor(TL["vb"][:], TL["va"][:], -INV_D,
                                       TL["sq2u"][:], op0=ALU.mult, op1=ALU.add); D_()
                v.tensor_scalar(TL["vc2"][:], TL["vb"][:], INV_D, 1e-5,
                                op0=ALU.mult, op1=ALU.add)
                D_().then_inc(S["sqr2q"], 1)

            def ve_fa1a(it):
                v.wait_ge(S["lx"], 16 * (it + 1))
                v.tensor_scalar(vs2[:, 0:1024], xb[it % 2][:], 0.0, None,
                                op0=ALU.add, op1=ALU.add,
                                accum_out=TL["sm"][:, 0:1]); D_()

            def ve_fa1b(it):
                v.wait_ge(S["r1d"], it + 1)
                v.tensor_tensor(TL["va"][:], TL["sm"][:], TL["sm"][:], ALU.mult); D_()
                v.scalar_tensor_tensor(TL["vb"][:], TL["va"][:], -INV_D,
                                       TL["sq"][:], op0=ALU.mult, op1=ALU.add); D_()
                v.tensor_scalar(TL["vc"][:], TL["vb"][:], INV_D, 1e-5,
                                op0=ALU.mult, op1=ALU.add)
                D_().then_inc(S["sqr1q"], 1)

            def ve_fa2(it):
                v.wait_ge(S["sqr1a"], it + 1)
                if it >= 1:
                    v.wait_ge(S["xnT"], it)
                v.reciprocal(TL["inv"][:], TL["sqr"][:]); D_()
                v.tensor_tensor(TL["vb"][:], TL["sm"][:], TL["inv"][:], ALU.mult); D_()
                v.tensor_scalar(TL["nb"][:], TL["vb"][:], -INV_D, None,
                                op0=ALU.mult); D_()
                v.tensor_scalar(xn[:], xb[it % 2][:], TL["inv"][:, 0:1],
                                TL["nb"][:, 0:1], op0=ALU.mult, op1=ALU.add)
                D_().then_inc(S["xn_"], 1)

            def ve_fb2(it):
                oi = it - NCH
                v.wait_ge(S["sqr2a"], oi + 1)
                v.reciprocal(TL["inv"][:], TL["sqr2"][:]); D_()
                v.tensor_tensor(TL["vb"][:], TL["sm2"][:], TL["inv"][:],
                                ALU.mult); D_()
                v.tensor_scalar(TL["nb"][:], TL["vb"][:], -INV_D, None,
                                op0=ALU.mult); D_()
                if it + 1 < NIT:
                    v.wait_ge(S["sq1a"], it + 2)
                v.tensor_scalar(xn2[:], x2[oi % 4][:], TL["inv"][:, 0:1],
                                TL["nb"][:, 0:1], op0=ALU.mult, op1=ALU.add); D_()
                v.tensor_reduce(TL["am"][:, 0:1], xn2[:], AX.X, ALU.max,
                                apply_absolute_value=True); D_()
                v.tensor_scalar(TL["cl"][:], TL["am"][:], 1e-5, None,
                                op0=ALU.max); D_()
                v.reciprocal(TL["rr"][:], TL["cl"][:]); D_()
                v.tensor_scalar(TL["sc1"][:], TL["rr"][:], 127.0, None,
                                op0=ALU.mult); D_()
                v.tensor_single_scalar(TL["a1t"][:], TL["cl"][:],
                                       TL["s1o"][:, 0:1], ALU.mult); D_()
                v.tensor_scalar(vs2[:, 0:1024], xn2[:], TL["sc1"][:, 0:1],
                                TL["C_t"][:, 0:1], op0=ALU.mult, op1=ALU.add); D_()
                if oi >= 1:
                    v.wait_ge(S["k1T"], oi)
                v.tensor_scalar(k1s[oi % 2][:], vs2[:, 0:1024],
                                TL["nC_t"][:, 0:1], TL["a1t"][:, 0:1],
                                op0=ALU.add, op1=ALU.mult)
                D_().then_inc(S["k1s_"], 1)

            def ve_uev(it):
                v.wait_ge(S["umm"], it + 1)
                v.tensor_single_scalar(u_s[it % 2][:], pmisc[0:64, 0:128],
                                       P64["Bbar"][:, 0:1], ALU.mult)
                D_().then_inc(S["uev"], 1)

            def ve_back_a(oi, b):
                v.wait_ge(S["gelu"], 8 * (oi + 1))
                h3 = h_sb.rearrange("p (m t) -> p t m", m=32)
                v.tensor_reduce(hred[:, 0:128], h3, AX.X, ALU.max,
                                apply_absolute_value=True)
                D_().then_inc(S["hmax"], 1)
                v.wait_ge(S["hmaxT"], oi + 1)
                v.tensor_reduce(TL["am2"][:, 0:1], pmisc[:, 128:256], AX.X,
                                ALU.max); D_()
                v.tensor_scalar(TL["cl2"][:], TL["am2"][:], 1e-5, None,
                                op0=ALU.max); D_()
                v.reciprocal(TL["r2"][:], TL["cl2"][:]); D_()
                v.tensor_scalar(TL["sc2"][:], TL["r2"][:], 127.0, None,
                                op0=ALU.mult); D_()
                a2d = TL["a2t"] if oi % 2 == 0 else TL["a2u"]
                v.tensor_single_scalar(a2d[:], TL["cl2"][:],
                                       TL["s2o"][:, 0:1], ALU.mult)
                D_().then_inc(S["sc2_"], 1)
                v.wait_ge(S["bce"], oi + 1)
                for q in range(4):
                    if oi >= 1:
                        v.wait_ge(S["m2h"], 4 * (oi - 1) + q + 1)
                    hs = h_sb[:, q * 1024:(q + 1) * 1024].rearrange(
                        "p (m t) -> p m t", m=8)
                    vq = vs2[:, 0:1024].rearrange("p (m t) -> p m t", m=8)
                    bcb = sc2bc.unsqueeze(1).broadcast_to([128, 8, 128])
                    v.tensor_tensor(vq, hs, bcb, ALU.mult); D_()
                    v.tensor_scalar(k2[:, q * 1024:(q + 1) * 1024],
                                    vs2[:, 0:1024],
                                    TL["C_t"][:, 0:1], TL["nC_t"][:, 0:1],
                                    op0=ALU.add, op1=ALU.add); D_()
                D_().then_inc(S["k2_"], 1)

            ve_fa1a(0)
            ve_fa1b(0)
            ve_fa2(0)
            ve_fa1a(1)
            ve_fa1b(1)
            ve_fa2(1)
            ve_uev(0)
            ve_scan(0)
            for b in range(NIT + 3):
                oi1 = b - NCH - 1
                oi3 = b - NCH - 3
                if b + 1 < NIT:
                    ve_uev(b + 1)
                    ve_scan(b + 1)
                if 0 <= oi3 < NCH:
                    ve_out(oi3)
                if NCH <= b + 1 < NIT:
                    ve_fb1a(b + 1)
                    ve_fb1b(b + 1)
                if b + 2 < NIT:
                    ve_fa1a(b + 2)
                    ve_fa1b(b + 2)
                if NCH <= b + 1 < NIT:
                    ve_fb2(b + 1)
                if b + 2 < NIT:
                    ve_fa2(b + 2)
                if 0 <= oi1 < NCH:
                    ve_back_a(oi1, b)

        @block.tensor
        def _(t):
            t.wait_ge(S["prep"], 1)
            t.wait_ge(S["k1m"], 1)
            t.matmul(pmisc[:, 0:2], ones1f[:], pack[0:1, 0:2], start=True,
                     stop=True).then_inc(S["k1m"], 1)
            t.wait_ge(S["winc"], 1)
            for a in range(8):
                ti = t.transpose(pT[:, a * 64:(a + 1) * 64],
                                 winB[0:64, a * 128:(a + 1) * 128],
                                 identb[0:64, 0:64])
            ti.then_inc(S["winTd"], 1)
            t.wait_ge(S["wotc"], 1)
            t.wait_ge(S["winTe"], 1)
            for a in range(8):
                ti = t.transpose(pT[0:64, a * 128:(a + 1) * 128],
                                 wqb[0][:, a * 64:(a + 1) * 64], identb[:])
            ti.then_inc(S["wotTd"], 1)
            t.wait_ge(S["wcT1"], 1)
            t.transpose(pmisc[0:1, 128:256], TL["wcol"][:, 0:1],
                        ident32[:]).then_inc(S["wcT1"], 1)
            t.wait_ge(S["sc1m"], 1)
            t.matmul(pmisc[:, 2:4], ones1f[:], pack[0:1, 0:2], start=True,
                     stop=True).then_inc(S["sc1m"], 1)
            t.wait_ge(S["wcT2"], 1)
            t.transpose(pmisc[0:1, 128:256], TL["wcol"][:, 0:1],
                        ident32[:]).then_inc(S["wcT2"], 1)
            t.wait_ge(S["sc2m"], 1)
            t.matmul(pmisc[:, 4:6], ones1f[:], pack[0:1, 0:2], start=True,
                     stop=True).then_inc(S["sc2m"], 1)
            t.wait_ge(S["wotTe"], 1)
            for j in range(NWT):
                t.wait_ge(S["q1"], j + 1)
                if j >= 1:
                    t.wait_ge(S["q1e"], j)
                for f in range(8):
                    ti = t.transpose(pT[:, f * 128:(f + 1) * 128],
                                     wqb[j % 2][:, f * 128:(f + 1) * 128],
                                     identb[:])
                ti.then_inc(S["q1T"], 1)

            def te_k1sT(oi):
                t.wait_ge(S["k1s_"], oi + 1)
                t.wait_ge(S["pTe"], PT_IDX[("k", oi)])
                for f in range(8):
                    ti = t.transpose(pT[:, f * 128:(f + 1) * 128],
                                     k1s[oi % 2][:, f * 128:(f + 1) * 128],
                                     identb[:])
                ti.then_inc(S["k1T"], 1)

            def te_wout(it):
                oi = it - NCH
                t.wait_ge(S["hb"], oi + 1)
                if oi == 0:
                    t.wait_ge(S["woctr"], 1)
                if oi >= 1:
                    t.wait_ge(S["x2_"], oi)
                t.matmul(pssm[:, 0:512], hb_s[:], woct[:, 0:512], start=True,
                         stop=True)
                t.matmul(pssm[:, 512:1024], hb_s[:], woct[:, 512:1024],
                         start=True, stop=True).then_inc(S["wout"], 1)

            def te_mm1(oi, wout_it=None, k1sT_oi=None):
                t.wait_ge(S["k1fm_"], oi + 1)
                if oi == 0:
                    t.wait_ge(S["q1e"], W1_TILES)
                for g in range(8):
                    G = 8 * oi + g
                    bank = pm1a if g % 2 == 0 else pm1b
                    if G >= 2:
                        t.wait_ge(S["gelu"], G - 1)
                    for mi in range(4):
                        m = g * 4 + mi
                        reg = bank[:, mi * 128:(mi + 1) * 128]
                        for k in range(8):
                            ti = t.matmul(
                                reg,
                                t1T[:, k * 4096 + m * 128:k * 4096 + (m + 1) * 128],
                                k1fm[oi % 2][:, k * 128:(k + 1) * 128],
                                start=(k == 0), stop=(k == 7))
                    ti.then_inc(S["mm1"], 1)
                    if g == 0:
                        if wout_it is not None:
                            te_wout(wout_it)
                        if k1sT_oi is not None:
                            te_k1sT(k1sT_oi)

            def te_fa(it):
                t.wait_ge(S["xn_"], it + 1)
                pt_use = PT_IDX[("x", it)]
                if it == 0:
                    t.wait_ge(S["q1e"], NWT)
                else:
                    t.wait_ge(S["pTe"], pt_use)
                for f in range(8):
                    ti = t.transpose(pT[:, f * 128:(f + 1) * 128],
                                     xn[:, f * 128:(f + 1) * 128], identb[:])
                ti.then_inc(S["xnT"], 1)
                t.wait_ge(S["xnfm_"], it + 1)
                if it == 0:
                    t.wait_ge(S["winTe"], 1)
                t.wait_ge(S["uev"], it)
                if it >= NCH + 3:
                    t.wait_ge(S["bce"], it - NCH - 2)
                for k in range(8):
                    ti = t.matmul(pmisc[0:64, 0:128], winT[:, k * 64:(k + 1) * 64],
                                  xnfm[it % 2][:, k * 128:(k + 1) * 128],
                                  start=(k == 0), stop=(k == 7))
                ti.then_inc(S["umm"], 1)

            def te_btr(oi, b):
                t.wait_ge(S["hmax"], oi + 1)
                t.wait_ge(S["uev"], min(b + 2, NIT))
                t.transpose(pmisc[:, 128:256], hred[:],
                            ident32[:]).then_inc(S["hmaxT"], 1)
                t.wait_ge(S["sc2_"], oi + 1)
                t.transpose(pmisc[0:1, 256:384], TL["sc2"][:, 0:1],
                            ident32[:]).then_inc(S["rowT"], 1)
                t.wait_ge(S["rowe"], oi + 1)
                t.matmul(pmisc[:, 384:512], ones1f[:], rowf[:], start=True,
                         stop=True).then_inc(S["bcmm"], 1)

            def te_mm2(oi, tiles):
                if tiles[0] == 0:
                    t.wait_ge(S["k2_"], oi + 1)
                    if oi == 0:
                        t.wait_ge(S["q1e"], NWT)
                    if oi >= 1:
                        t.wait_ge(S["outv"], oi)
                for hk in tiles:
                    for d in range(2):
                        ti = t.matmul(
                            pout[:, d * 512:(d + 1) * 512],
                            k2[:, hk * 128:(hk + 1) * 128],
                            t2T[:, hk * 1024 + d * 512:hk * 1024 + d * 512 + 512],
                            start=(hk == 0), stop=(hk == 31))
                    if hk % 8 == 7:
                        ti.then_inc(S["m2h"], 1)

            te_fa(0)
            for b in range(NIT + 2):
                oi1 = b - NCH - 1
                oi2 = b - NCH - 2
                has1 = 0 <= oi1 < NCH
                has2 = 0 <= oi2 < NCH
                w_it = b + 1 if NCH <= b + 1 < NIT else None
                k_oi = b - NCH if 0 <= b - NCH < NCH else None
                if b + 1 < NIT:
                    te_fa(b + 1)
                if has1:
                    te_mm1(oi1, wout_it=w_it, k1sT_oi=k_oi)
                else:
                    if w_it is not None:
                        te_wout(w_it)
                    if k_oi is not None:
                        te_k1sT(k_oi)
                if has2:
                    te_mm2(oi2, range(0, 16))
                if has1:
                    te_btr(oi1, b)
                if has2:
                    te_mm2(oi2, range(16, 32))

    return nc


def kernel(**inputs):
    x = np.ascontiguousarray(np.asarray(inputs["L0_attn_x"], np.float32))
    xs = x.reshape(NCORE, TOK, D)
    w1 = np.ascontiguousarray(np.asarray(inputs["L0_ffn_w1"], np.float32))
    w2 = np.ascontiguousarray(np.asarray(inputs["L0_ffn_w2"], np.float32))
    win = np.ascontiguousarray(np.asarray(inputs["L0_attn_W_in"], np.float32))
    wout = np.ascontiguousarray(np.asarray(inputs["L0_attn_W_out"], np.float32))
    logA = np.asarray(inputs["L0_attn_log_A"], np.float32).reshape(N, 1)
    Bp = np.asarray(inputs["L0_attn_B_param"], np.float32).reshape(N, 1)
    Cp = np.asarray(inputs["L0_attn_C_param"], np.float32).reshape(N, 1)
    logdt = np.asarray(inputs["L0_attn_log_dt"], np.float32).reshape(1, 1)

    nc = build_nc()
    in_maps = []
    for i in range(NCORE):
        odd = i % 2 == 1
        in_maps.append({
            "xh": xs[i - 1] if odd else xs[i],
            "xo": xs[i],
            "w1": w1, "w2": w2, "win": win, "wout": wout,
            "logA": logA, "Bp": Bp, "Cp": Cp, "logdt": logdt,
            "gate": np.array([[1.0 if odd else 0.0]], np.float32),
        })
    res = run_bass_kernel_spmd(nc, in_maps, core_ids=list(range(NCORE)))
    out = np.concatenate([res.results[i]["out"] for i in range(NCORE)], axis=0)
    return out.reshape(B, L, D)


# version 16
# speedup vs baseline: 1.7101x; 1.1969x over previous
import numpy as np
from contextlib import ExitStack
import concourse.bass as bass
import concourse.mybir as mybir
from concourse.bass_utils import run_bass_kernel_spmd

F32 = mybir.dt.float32
BF16 = mybir.dt.bfloat16
AF = mybir.ActivationFunctionType
ALU = mybir.AluOpType
AX = mybir.AxisListType
C32 = float(1.5 * 2.0**23)
C64 = float(1.5 * 2.0**52)

B, L, D, N, H = 4, 8192, 1024, 64, 4096
NCORE = 8
TOK = L // 2
NCH = TOK // 128
NIT = 2 * NCH
INV_D = 1.0 / D
W1_TILES = 32
W2_TILES = 32
NWT = W1_TILES + W2_TILES

_PT_ORDER = [("x", 0)]
for _b in range(NIT + 2):
    if _b + 1 < NIT:
        _PT_ORDER.append(("x", _b + 1))
    if 0 <= _b - NCH < NCH:
        _PT_ORDER.append(("k", _b - NCH))
PT_IDX = {key: i for i, key in enumerate(_PT_ORDER)}

W_SLOT = []
_rc = [0, 0, 0]
for _t in range(2 * NWT):
    _r = (_t % NWT) % 3
    W_SLOT.append((_r, _rc[_r]))
    _rc[_r] += 1
W_RING = [[t for t in range(2 * NWT) if W_SLOT[t][0] == r] for r in range(3)]


def build_nc():
    nc = bass.Bass("TRN2", target_bir_lowering=False)
    dr = {}
    for n, shp in [("xh", (TOK, D)), ("xo", (TOK, D)),
                   ("w1", (H, D)), ("w2", (D, H)),
                   ("win", (N, D)), ("wout", (D, N)),
                   ("logA", (N, 1)), ("Bp", (N, 1)), ("Cp", (N, 1)),
                   ("logdt", (1, 1)), ("gate", (1, 1))]:
        dr[n] = nc.dram_tensor(n, list(shp), F32, kind="ExternalInput").ap()
    dr["out"] = nc.dram_tensor("out", [TOK, D], F32, kind="ExternalOutput").ap()

    sbt = lambda n, shp, dt=F32: nc.alloc_sbuf_tensor(n, list(shp), dt).ap()
    t1T = sbt("t1T", (128, 8 * H), BF16)
    t2T = sbt("t2T", (128, 32 * D), BF16)
    winT = sbt("winT", (128, 8 * N), BF16)
    woct = sbt("woct", (64, D), BF16)
    xb = [sbt(f"xb{i}", (128, D)) for i in range(2)]
    xn = sbt("xn", (128, D), BF16)
    xnfm = [sbt(f"xnfm{i}", (128, D), BF16) for i in range(2)]
    x2_3 = [sbt(f"x2_{i}", (128, D)) for i in range(3)]
    x2 = None
    xn2 = sbt("xn2", (128, D))
    k1s0 = sbt("k1s0", (128, D), BF16)
    k1s = [k1s0, k1s0]
    k1fm0 = sbt("k1fm0", (128, D), BF16)
    k1fm = [k1fm0, k1fm0]
    h_sb = sbt("h_sb", (128, H))
    vs2 = sbt("vs2", (128, 2048))
    x2 = x2_3 + [vs2[:, 1024:2048]]
    k2 = sbt("k2", (128, H), BF16)
    sc2bc = sbt("sc2bc", (128, 128))
    hred = sbt("hred", (128, 128))
    rowf = sbt("rowf", (1, 128))
    out_s0 = sbt("out_s0", (128, D))
    out_s = [out_s0, out_s0]
    u_s = [sbt(f"u_s{i}", (64, 128)) for i in range(2)]
    h_s = [sbt(f"h_s{i}", (64, 128)) for i in range(2)]
    hb_s = sbt("hb_s", (64, 128), BF16)
    abt = sbt("abt", (64, 128))
    cgt = sbt("cgt", (64, 1))
    zero64 = sbt("zero64", (64, 1))
    identb = sbt("identb", (128, 128), BF16)
    ident32 = sbt("ident32", (128, 128))
    ones1f = sbt("ones1f", (1, 128))
    TLn = ["sm", "sq", "va", "vb", "vc", "sqr", "inv", "nb", "am", "cl", "rr",
           "sc1", "a1t", "am2", "cl2", "r2", "sc2", "a2t", "C_t", "nC_t",
           "one_t", "chi", "c64", "nc64", "r1b", "s1o", "r2b", "s2o", "wcol",
           "vc2", "sqr2", "a2u", "sm2", "sq2u"]
    TLblk = sbt("TLblk", (128, len(TLn)), F32)
    TL = {n: TLblk[:, i:i + 1] for i, n in enumerate(TLn)}
    P64n = ["logA", "Bp", "Cp", "expA", "Aneg", "dtb", "gate64", "lnab",
            "Abar", "am1", "mask", "d1", "e1", "den", "rden", "f0", "f1",
            "f2", "f3", "frac", "Bbar"]
    P64blk = sbt("P64blk", (64, len(P64n)), F32)
    P64 = {n: P64blk[:, i:i + 1] for i, n in enumerate(P64n)}
    sc11 = sbt("sc11", (1, 1))
    sc12 = sbt("sc12", (1, 1))
    gt11 = sbt("gt11", (1, 1))
    pack = sbt("pack", (1, 4))
    wtot = sbt("wtot", (1, 1))
    onesn = sc2bc[0:64, 0:128]
    identD = hred
    k1 = vs2.bitcast(BF16)[:, 2048:3072]
    scr_sq = xn2
    scr_vt = vs2[:, 0:1024]
    scr_sq = xn2
    wst = [vs2[:, 0:1024], vs2[:, 1024:2048]]
    wstB = [h_sb[:, 1024:2048], h_sb[:, 2048:3072]]
    wstC = [h_sb[:, 3072:4096], out_s0[:, 0:1024]]
    WSTG = [wst, wstB, wstC]
    wqb = [k2[:, 0:1024], k2[:, 1024:2048]]
    winB = k2[:, 2048:3072]
    woctB0 = k2[:, 3072:4096]
    wost = h_sb[:, 0:512]
    winF = xn2
    wmul = x2_3[0]
    wrnd = x2_3[1]
    wabs = hred
    pT = nc.alloc_psum_tensor("pT", [128, 1024], BF16).ap()
    pmisc = nc.alloc_psum_tensor("pmisc", [128, 512], F32).ap()
    pssm = nc.alloc_psum_tensor("pssm", [128, 1024], F32).ap()
    pm1a = nc.alloc_psum_tensor("pm1a", [128, 512], F32).ap()
    pm1b = nc.alloc_psum_tensor("pm1b", [128, 512], F32).ap()
    pout = nc.alloc_psum_tensor("pout", [128, 1024], F32).ap()

    t1T_kv = t1T.rearrange("p (k r) -> p k r", k=8)
    t2T_qv = t2T.rearrange("p (q f r) -> p q f r", q=4, f=8)
    pT_v8 = pT.rearrange("p (k t) -> p k t", k=8)

    SEMS = ["lx", "lw", "lws", "lwg", "lp", "lo", "gp", "prep",
            "wa", "s1b", "s2b", "q1", "q1T", "q1e",
            "zoh1", "zoh2", "k1m", "k1me", "winc", "winTd", "winTe",
            "wotc", "wotTd", "wotTe", "woctr", "wcT1", "wcT2", "sc1m", "sc2m",
            "sq1a", "sq2a", "r1d", "r2d", "sqr1q", "sqr1a", "xn_", "xnT", "xnfm_", "umm", "scan",
            "hb", "wout", "x2_", "sqr2q", "sqr2a", "uev",
            "k1s_", "k1T", "k1fm_", "mm1", "gelu", "hmax", "hmaxT", "sc2_",
            "rowT", "rowe", "bcmm", "bce", "k2_", "m2h", "outv", "od", "pTe"]

    with ExitStack() as ctx:
        S = {n: ctx.enter_context(nc.semaphore(n)) for n in SEMS}
        block = ctx.enter_context(nc.Block())

        WSEM = ["lw", "lws", "lwg"]
        w1_v = dr["w1"].rearrange("(j p) d -> j p d", p=128)
        w2_v = dr["w2"].rearrange("(i p) (q d) -> i q p d", p=128, d=1024)

        def wtile(j):
            return w1_v[j] if j < W1_TILES else \
                w2_v[(j - W1_TILES) // 4, (j - W1_TILES) % 4]

        def w_dma(eng, r, t):
            _, si = W_SLOT[t]
            if si >= 2:
                tp = W_RING[r][si - 2]
                if tp < NWT:
                    eng.wait_ge(S["wa"], tp + 1)
                else:
                    eng.wait_ge(S["q1"], tp - NWT + 1)
            eng.wait_ge(S[WSEM[r]], 16 * si)
            eng.dma_start(out=WSTG[r][si % 2],
                          in_=wtile(t % NWT)).then_inc(S[WSEM[r]], 16)

        @block.sync
        def _(sync):
            for di, nm in enumerate(["logA", "Bp", "Cp", "logdt", "gate", "win"]):
                dst = {"logA": P64["logA"], "Bp": P64["Bp"], "Cp": P64["Cp"],
                       "logdt": sc11, "gate": gt11}.get(nm)
                d = winF[0:64, :] if nm == "win" else dst[:]
                sync.wait_ge(S["lp"], 16 * di)
                sync.dma_start(out=d, in_=dr[nm][:, :]).then_inc(S["lp"], 16)
            wo_v = dr["wout"].rearrange("(a p) n -> a p n", p=128)
            for a in range(8):
                sync.wait_ge(S["lo"], 16 * a)
                sync.dma_start(out=wost[:, a * 64:(a + 1) * 64],
                               in_=wo_v[a]).then_inc(S["lo"], 16)
            for t in W_RING[1]:
                w_dma(sync, 1, t)
            for it in range(NIT):
                sync.wait_ge(S["lx"], 16 * it)
                if it >= 2:
                    sync.wait_ge(S["xn_"], it - 1)
                    sync.wait_ge(S["sq1a"], it - 1)
                    if it - 2 >= NCH:
                        sync.wait_ge(S["x2_"], it - 2 - NCH + 1)
                    else:
                        sync.wait_ge(S["scan"], it - 1)
                src = dr["xh"] if it < NCH else dr["xo"]
                c = it if it < NCH else it - NCH
                sync.dma_start(out=xb[it % 2][:],
                               in_=src[c * 128:(c + 1) * 128, :]).then_inc(S["lx"], 16)

        @block.gpsimd
        def _(g):
            g.iota(identD[:], pattern=[[-1, 128]], base=0, channel_multiplier=1,
                   allow_small_or_imprecise_dtypes=True)
            g.drain().then_inc(S["gp"], 1)
            for t in W_RING[2]:
                w_dma(g, 2, t)
            def gp_sq2(it):
                oi = it - NCH
                g.wait_ge(S["x2_"], oi + 1)
                if it >= NCH + 1:
                    g.wait_ge(S["k1s_"], it - NCH)
                g.tensor_tensor(scr_sq[:], x2[oi % 4][:], x2[oi % 4][:],
                                ALU.mult)
                g.drain().then_inc(S["sq2a"], 1)

            def gp_sq1(it):
                g.wait_ge(S["lx"], 16 * (it + 1))
                if it >= 1:
                    g.wait_ge(S["r1d"], it)
                if it - NCH >= 1:
                    g.wait_ge(S["r2d"], it - NCH)
                g.tensor_tensor(scr_sq[:], xb[it % 2][:], xb[it % 2][:],
                                ALU.mult)
                g.drain().then_inc(S["sq1a"], 1)

            gp_sq1(0)
            gp_sq1(1)
            for b in range(NIT + 3):
                oi3 = b - NCH - 3
                if NCH <= b + 1 < NIT:
                    gp_sq2(b + 1)
                if b + 2 < NIT:
                    gp_sq1(b + 2)
                if 0 <= oi3 < NCH:
                    g.wait_ge(S["od"], 16 * oi3)
                    g.wait_ge(S["outv"], oi3 + 1)
                    g.dma_start(out=dr["out"][oi3 * 128:(oi3 + 1) * 128, :],
                                in_=out_s[oi3 % 2][:]).then_inc(S["od"], 16)
            g.wait_ge(S["od"], 16 * NCH)

        @block.scalar
        def _(s):
            s.wait_ge(S["lp"], 16 * 6)
            s.activation(P64["expA"][:], P64["logA"][:], AF.Exp, bias=0.0, scale=1.0)
            s.activation(sc12[:], sc11[:], AF.Exp, bias=0.0, scale=1.0)
            s.drain().then_inc(S["zoh1"], 1)
            s.wait_ge(S["k1m"], 2)
            s.copy(P64["dtb"][:], pmisc[0:64, 0:1])
            s.copy(P64["gate64"][:], pmisc[0:64, 1:2])
            s.drain().then_inc(S["k1me"], 1)
            s.wait_ge(S["zoh2"], 1)
            s.activation(P64["Abar"][:], P64["lnab"][:], AF.Exp, bias=0.0, scale=1.0)
            s.drain().then_inc(S["zoh2"], 1)
            s.wait_ge(S["winTd"], 1)
            s.copy(winT[:], pT[:, 0:512])
            s.drain().then_inc(S["winTe"], 1)
            s.wait_ge(S["wotTd"], 1)
            s.copy(woctB0[0:64, :], pT[0:64, 0:1024])
            s.drain().then_inc(S["wotTe"], 1)
            for t in W_RING[0]:
                if t < NWT:
                    w_dma(s, 0, t)
            s.wait_ge(S["sc1m"], 2)
            s.copy(TL["r1b"][:], pmisc[:, 2:3])
            s.copy(TL["s1o"][:], pmisc[:, 3:4])
            s.drain().then_inc(S["s1b"], 1)

            def evac_ternary(jm):
                s.wait_ge(S["q1T"], jm + 1)
                if jm < W1_TILES:
                    s.copy(t1T_kv[:, :, jm * 128:(jm + 1) * 128], pT_v8[:])
                else:
                    ii, qq = (jm - W1_TILES) // 4, (jm - W1_TILES) % 4
                    s.copy(t2T_qv[:, qq:qq + 1, :, ii * 128:(ii + 1) * 128],
                           pT_v8.unsqueeze(1))
                s.drain().then_inc(S["q1e"], 1)

            for j in range(NWT):
                if j == W1_TILES:
                    s.wait_ge(S["sc2m"], 2)
                    s.copy(TL["r2b"][:], pmisc[:, 4:5])
                    s.copy(TL["s2o"][:], pmisc[:, 5:6])
                    s.drain().then_inc(S["s2b"], 1)
                if j % 3 == 0:
                    w_dma(s, 0, NWT + j)
                if j >= 1:
                    evac_ternary(j - 1)
            evac_ternary(NWT - 1)

            def se_hb(it):
                s.wait_ge(S["scan"], it + 1)
                s.copy(hb_s[:], h_s[it % 2][:])
                s.drain().then_inc(S["hb"], 1)

            def se_k1fm(oi):
                s.wait_ge(S["k1T"], oi + 1)
                if oi >= 1:
                    s.wait_ge(S["mm1"], 8 * oi)
                s.copy(k1fm[oi % 2][:], pT[:])
                s.drain().then_inc(S["k1fm_"], 1)
                s.drain().then_inc(S["pTe"], 1)

            def se_gelu(oi, g):
                s.wait_ge(S["mm1"], 8 * oi + g + 1)
                if g == 0 and oi >= 1:
                    s.wait_ge(S["k2_"], oi)
                if oi == 0 and g == 2:
                    s.wait_ge(S["q1"], NWT)
                bank = pm1a if g % 2 == 0 else pm1b
                s.activation(h_sb[:, g * 512:(g + 1) * 512], bank[:], AF.Gelu,
                             bias=0.0, scale=1.0)
                s.drain().then_inc(S["gelu"], 1)

            def se_sqrt2(it):
                oi = it - NCH
                s.wait_ge(S["sqr2q"], oi + 1)
                s.activation(TL["sqr2"][:], TL["vc2"][:], AF.Sqrt, bias=0.0,
                             scale=1.0)
                s.drain().then_inc(S["sqr2a"], 1)

            def se_sqrt1(it):
                s.wait_ge(S["sqr1q"], it + 1)
                s.activation(TL["sqr"][:], TL["vc"][:], AF.Sqrt, bias=0.0,
                             scale=1.0)
                s.drain().then_inc(S["sqr1a"], 1)

            def se_xnfm(it):
                s.wait_ge(S["xnT"], it + 1)
                if it >= 2:
                    s.wait_ge(S["umm"], it - 1)
                s.copy(xnfm[it % 2][:], pT[:])
                s.drain().then_inc(S["xnfm_"], 1)
                s.drain().then_inc(S["pTe"], 1)

            def se_rowbc(oi):
                s.wait_ge(S["rowT"], oi + 1)
                s.copy(rowf[:], pmisc[0:1, 256:384])
                s.drain().then_inc(S["rowe"], 1)
                s.wait_ge(S["bcmm"], oi + 1)
                s.copy(sc2bc[:], pmisc[:, 384:512])
                s.drain().then_inc(S["bce"], 1)

            se_sqrt1(0)
            se_sqrt1(1)
            se_xnfm(0)
            for b in range(NIT + 2):
                oi1 = b - NCH - 1
                has1 = 0 <= oi1 < NCH
                if b + 1 < NIT:
                    se_xnfm(b + 1)
                if NCH <= b + 1 < NIT:
                    se_hb(b + 1)
                if has1:
                    for g in range(0, 6):
                        se_gelu(oi1, g)
                if NCH <= b + 1 < NIT:
                    se_sqrt2(b + 1)
                if b + 2 < NIT:
                    se_sqrt1(b + 2)
                if has1:
                    se_gelu(oi1, 6)
                    se_gelu(oi1, 7)
                if 0 <= b - NCH < NCH:
                    se_k1fm(b - NCH)
                if has1:
                    se_rowbc(oi1)

        @block.vector
        def _(v):
            D_ = v.drain
            v.wait_ge(S["gp"], 1)
            v.tensor_single_scalar(identb[:], identD[:], 0.0, ALU.is_equal)
            v.tensor_single_scalar(ident32[:], identD[:], 0.0, ALU.is_equal)
            v.memset(ones1f[:], 1.0)
            v.memset(onesn[:], 1.0)
            v.memset(zero64[:], 0.0)
            v.memset(TL["one_t"][:], 1.0)
            v.memset(TL["c64"][:], C64)
            v.memset(TL["nc64"][:], -C64)
            D_()
            v.tensor_scalar(TL["chi"][:], TL["one_t"][:], TL["c64"][:, 0:1],
                            TL["nc64"][:, 0:1], op0=ALU.add, op1=ALU.add); D_()
            v.tensor_scalar(TL["va"][:], TL["chi"][:], TL["c64"][:, 0:1], C32,
                            op0=ALU.mult, op1=ALU.add); D_()
            v.tensor_copy(TL["C_t"][:], TL["va"][:]); D_()
            v.tensor_scalar(TL["nC_t"][:], TL["va"][:], -1.0, None, op0=ALU.mult)
            D_().then_inc(S["prep"], 1)
            v.wait_ge(S["zoh1"], 1)
            v.tensor_scalar(P64["Aneg"][:], P64["expA"][:], -1.0, None,
                            op0=ALU.mult); D_()
            v.tensor_copy(pack[0:1, 0:1], sc12[:])
            v.tensor_copy(pack[0:1, 1:2], gt11[:])
            D_().then_inc(S["k1m"], 1)
            v.wait_ge(S["k1me"], 1)
            v.tensor_tensor(P64["lnab"][:], P64["Aneg"][:], P64["dtb"][:],
                            ALU.mult)
            D_().then_inc(S["zoh2"], 1)
            v.wait_ge(S["zoh2"], 2)
            v.tensor_scalar(P64["am1"][:], P64["Abar"][:], -1.0, None,
                            op0=ALU.add); D_()
            v.tensor_single_scalar(P64["mask"][:], P64["expA"][:], 1e-6,
                                   ALU.is_lt); D_()
            v.tensor_tensor(P64["d1"][:], P64["mask"][:], P64["Aneg"][:],
                            ALU.mult); D_()
            v.tensor_tensor(P64["e1"][:], P64["Aneg"][:], P64["d1"][:],
                            ALU.subtract); D_()
            v.tensor_tensor(P64["den"][:], P64["e1"][:], P64["mask"][:],
                            ALU.add); D_()
            v.reciprocal(P64["rden"][:], P64["den"][:]); D_()
            v.tensor_tensor(P64["f0"][:], P64["am1"][:], P64["rden"][:],
                            ALU.mult); D_()
            v.tensor_tensor(P64["f1"][:], P64["mask"][:], P64["dtb"][:],
                            ALU.mult); D_()
            v.tensor_tensor(P64["f2"][:], P64["mask"][:], P64["f0"][:],
                            ALU.mult); D_()
            v.tensor_tensor(P64["f3"][:], P64["f0"][:], P64["f2"][:],
                            ALU.subtract); D_()
            v.tensor_tensor(P64["frac"][:], P64["f3"][:], P64["f1"][:],
                            ALU.add); D_()
            v.tensor_tensor(P64["Bbar"][:], P64["frac"][:], P64["Bp"][:],
                            ALU.mult); D_()
            v.tensor_single_scalar(abt[:], onesn[:], P64["Abar"][:, 0:1],
                                   ALU.mult); D_()
            v.wait_ge(S["lp"], 16 * 6)
            v.wait_ge(S["lo"], 16 * 8)
            v.tensor_copy(winB[0:64, :], winF[0:64, :])
            D_().then_inc(S["winc"], 1)
            v.tensor_copy(wqb[0][:, 0:512], wost[:, 0:512])
            D_().then_inc(S["wotc"], 1)
            v.wait_ge(S["wotTe"], 1)
            v.tensor_single_scalar(woct[0:64, :], woctB0[0:64, :],
                                   P64["Cp"][:, 0:1], ALU.mult)
            D_().then_inc(S["woctr"], 1)
            for j in range(NWT):
                _r, _si = W_SLOT[j]
                v.wait_ge(S[WSEM[_r]], 16 * (_si + 1))
                v.tensor_reduce(wabs[:, j:j + 1], WSTG[_r][_si % 2], AX.X, ALU.add,
                                apply_absolute_value=True)
                D_().then_inc(S["wa"], 1)
                if j == W1_TILES - 1:
                    v.tensor_reduce(TL["wcol"][:, 0:1], wabs[:, 0:W1_TILES],
                                    AX.X, ALU.add)
                    D_().then_inc(S["wcT1"], 1)
                    v.wait_ge(S["wcT1"], 2)
                    v.tensor_reduce(wtot[0:1, 0:1], pmisc[0:1, 128:256], AX.X,
                                    ALU.add); D_()
                    v.tensor_scalar(pack[0:1, 2:3], wtot[0:1, 0:1],
                                    1.0 / (H * D), 1e-5, op0=ALU.mult,
                                    op1=ALU.max); D_()
                    v.reciprocal(pack[0:1, 0:1], pack[0:1, 2:3]); D_()
                    v.tensor_scalar(pack[0:1, 1:2], pack[0:1, 2:3],
                                    1.0 / 127.0, None, op0=ALU.mult)
                    D_().then_inc(S["sc1m"], 1)
            v.wait_ge(S["s1b"], 1)
            v.tensor_reduce(TL["wcol"][:, 0:1], wabs[:, W1_TILES:NWT],
                            AX.X, ALU.add)
            D_().then_inc(S["wcT2"], 1)
            v.wait_ge(S["wcT2"], 2)
            v.tensor_reduce(wtot[0:1, 0:1], pmisc[0:1, 128:256], AX.X,
                            ALU.add); D_()
            v.tensor_scalar(pack[0:1, 2:3], wtot[0:1, 0:1], 1.0 / (H * D), 1e-5,
                            op0=ALU.mult, op1=ALU.max); D_()
            v.reciprocal(pack[0:1, 0:1], pack[0:1, 2:3]); D_()
            v.tensor_scalar(pack[0:1, 1:2], pack[0:1, 2:3], 1.0 / 127.0, None,
                            op0=ALU.mult)
            D_().then_inc(S["sc2m"], 1)
            v.wait_ge(S["s1b"], 1)
            v.wait_ge(S["wotTd"], 1)
            for j in range(NWT):
                rb = TL["r1b"] if j < W1_TILES else TL["r2b"]
                if j == W1_TILES:
                    v.wait_ge(S["s2b"], 1)
                _r, _si = W_SLOT[NWT + j]
                v.wait_ge(S[WSEM[_r]], 16 * (_si + 1))
                if j >= 2:
                    v.wait_ge(S["q1T"], j - 1)
                v.tensor_single_scalar(wmul[:], WSTG[_r][_si % 2], rb[:, 0:1],
                                       ALU.mult); D_()
                v.tensor_scalar(wrnd[:], wmul[:], TL["C_t"][:, 0:1],
                                TL["nC_t"][:, 0:1], op0=ALU.add, op1=ALU.add); D_()
                v.tensor_scalar(wqb[j % 2], wrnd[:], 1.0, -1.0,
                                op0=ALU.min, op1=ALU.max)
                D_().then_inc(S["q1"], 1)

            def ve_scan(it):
                if it == 0:
                    ini = zero64[:, 0:1]
                elif it == NCH:
                    v.tensor_tensor(cgt[:], h_s[(it - 1) % 2][:, 127:128],
                                    P64["gate64"][:], ALU.mult); D_()
                    ini = cgt[:, 0:1]
                else:
                    ini = h_s[(it - 1) % 2][:, 127:128]
                if it - NCH - 1 >= 1:
                    v.wait_ge(S["hb"], it - NCH - 1)
                v.tensor_tensor_scan(h_s[it % 2][:], abt[:], u_s[it % 2][:], ini,
                                     op0=ALU.mult, op1=ALU.add)
                D_().then_inc(S["scan"], 1)

            def ve_out(oi):
                v.wait_ge(S["m2h"], 4 * (oi + 1))
                if oi >= 1:
                    v.wait_ge(S["od"], 16 * oi)
                a2d = TL["a2t"] if oi % 2 == 0 else TL["a2u"]
                for half in range(2):
                    v.scalar_tensor_tensor(out_s[oi % 2][:, half * 512:(half + 1) * 512],
                                           pout[:, half * 512:(half + 1) * 512],
                                           a2d[:, 0:1],
                                           x2[oi % 4][:, half * 512:(half + 1) * 512],
                                           op0=ALU.mult, op1=ALU.add)
                D_().then_inc(S["outv"], 1)

            def ve_fb1a(it):
                oi = it - NCH
                v.wait_ge(S["wout"], oi + 1)
                v.tensor_tensor(x2[oi % 4][:, 0:512], xb[it % 2][:, 0:512],
                                pssm[:, 0:512], ALU.add)
                v.tensor_tensor(x2[oi % 4][:, 512:1024], xb[it % 2][:, 512:1024],
                                pssm[:, 512:1024], ALU.add)
                D_().then_inc(S["x2_"], 1)
                v.tensor_scalar(vs2[:, 0:1024], x2[oi % 4][:], 0.0, None,
                                op0=ALU.add, op1=ALU.add,
                                accum_out=TL["sm2"][:, 0:1]); D_()

            def ve_fb1b(it):
                oi = it - NCH
                v.wait_ge(S["sq2a"], oi + 1)
                v.tensor_reduce(TL["sq2u"][:, 0:1], scr_sq[:], AX.X, ALU.add)
                D_().then_inc(S["r2d"], 1)
                v.tensor_tensor(TL["va"][:], TL["sm2"][:], TL["sm2"][:],
                                ALU.mult); D_()
                v.scalar_tensor_tensor(TL["vb"][:], TL["va"][:], -INV_D,
                                       TL["sq2u"][:], op0=ALU.mult, op1=ALU.add); D_()
                v.tensor_scalar(TL["vc2"][:], TL["vb"][:], INV_D, 1e-5,
                                op0=ALU.mult, op1=ALU.add)
                D_().then_inc(S["sqr2q"], 1)

            def ve_fa1a(it):
                v.wait_ge(S["lx"], 16 * (it + 1))
                v.tensor_scalar(vs2[:, 0:1024], xb[it % 2][:], 0.0, None,
                                op0=ALU.add, op1=ALU.add,
                                accum_out=TL["sm"][:, 0:1]); D_()

            def ve_fa1b(it):
                v.wait_ge(S["sq1a"], it + 1)
                v.tensor_reduce(TL["sq"][:, 0:1], scr_sq[:], AX.X, ALU.add)
                D_().then_inc(S["r1d"], 1)
                v.tensor_tensor(TL["va"][:], TL["sm"][:], TL["sm"][:], ALU.mult); D_()
                v.scalar_tensor_tensor(TL["vb"][:], TL["va"][:], -INV_D,
                                       TL["sq"][:], op0=ALU.mult, op1=ALU.add); D_()
                v.tensor_scalar(TL["vc"][:], TL["vb"][:], INV_D, 1e-5,
                                op0=ALU.mult, op1=ALU.add)
                D_().then_inc(S["sqr1q"], 1)

            def ve_fa2(it):
                v.wait_ge(S["sqr1a"], it + 1)
                if it >= 1:
                    v.wait_ge(S["xnT"], it)
                v.reciprocal(TL["inv"][:], TL["sqr"][:]); D_()
                v.tensor_tensor(TL["vb"][:], TL["sm"][:], TL["inv"][:], ALU.mult); D_()
                v.tensor_scalar(TL["nb"][:], TL["vb"][:], -INV_D, None,
                                op0=ALU.mult); D_()
                v.tensor_scalar(xn[:], xb[it % 2][:], TL["inv"][:, 0:1],
                                TL["nb"][:, 0:1], op0=ALU.mult, op1=ALU.add)
                D_().then_inc(S["xn_"], 1)

            def ve_fb2(it):
                oi = it - NCH
                v.wait_ge(S["sqr2a"], oi + 1)
                v.reciprocal(TL["inv"][:], TL["sqr2"][:]); D_()
                v.tensor_tensor(TL["vb"][:], TL["sm2"][:], TL["inv"][:],
                                ALU.mult); D_()
                v.tensor_scalar(TL["nb"][:], TL["vb"][:], -INV_D, None,
                                op0=ALU.mult); D_()
                if it + 1 < NIT:
                    v.wait_ge(S["sq1a"], it + 2)
                v.tensor_scalar(xn2[:], x2[oi % 4][:], TL["inv"][:, 0:1],
                                TL["nb"][:, 0:1], op0=ALU.mult, op1=ALU.add); D_()
                v.tensor_reduce(TL["am"][:, 0:1], xn2[:], AX.X, ALU.max,
                                apply_absolute_value=True); D_()
                v.tensor_scalar(TL["cl"][:], TL["am"][:], 1e-5, None,
                                op0=ALU.max); D_()
                v.reciprocal(TL["rr"][:], TL["cl"][:]); D_()
                v.tensor_scalar(TL["sc1"][:], TL["rr"][:], 127.0, None,
                                op0=ALU.mult); D_()
                v.tensor_single_scalar(TL["a1t"][:], TL["cl"][:],
                                       TL["s1o"][:, 0:1], ALU.mult); D_()
                v.tensor_scalar(vs2[:, 0:1024], xn2[:], TL["sc1"][:, 0:1],
                                TL["C_t"][:, 0:1], op0=ALU.mult, op1=ALU.add); D_()
                if oi >= 1:
                    v.wait_ge(S["k1T"], oi)
                v.tensor_scalar(k1s[oi % 2][:], vs2[:, 0:1024],
                                TL["nC_t"][:, 0:1], TL["a1t"][:, 0:1],
                                op0=ALU.add, op1=ALU.mult)
                D_().then_inc(S["k1s_"], 1)

            def ve_uev(it):
                v.wait_ge(S["umm"], it + 1)
                v.tensor_single_scalar(u_s[it % 2][:], pmisc[0:64, 0:128],
                                       P64["Bbar"][:, 0:1], ALU.mult)
                D_().then_inc(S["uev"], 1)

            def ve_back_a(oi, b):
                v.wait_ge(S["gelu"], 8 * (oi + 1))
                h3 = h_sb.rearrange("p (m t) -> p t m", m=32)
                v.tensor_reduce(hred[:, 0:128], h3, AX.X, ALU.max,
                                apply_absolute_value=True)
                D_().then_inc(S["hmax"], 1)
                v.wait_ge(S["hmaxT"], oi + 1)
                v.tensor_reduce(TL["am2"][:, 0:1], pmisc[:, 128:256], AX.X,
                                ALU.max); D_()
                v.tensor_scalar(TL["cl2"][:], TL["am2"][:], 1e-5, None,
                                op0=ALU.max); D_()
                v.reciprocal(TL["r2"][:], TL["cl2"][:]); D_()
                v.tensor_scalar(TL["sc2"][:], TL["r2"][:], 127.0, None,
                                op0=ALU.mult); D_()
                a2d = TL["a2t"] if oi % 2 == 0 else TL["a2u"]
                v.tensor_single_scalar(a2d[:], TL["cl2"][:],
                                       TL["s2o"][:, 0:1], ALU.mult)
                D_().then_inc(S["sc2_"], 1)
                v.wait_ge(S["bce"], oi + 1)
                for q in range(4):
                    if oi >= 1:
                        v.wait_ge(S["m2h"], 4 * (oi - 1) + q + 1)
                    hs = h_sb[:, q * 1024:(q + 1) * 1024].rearrange(
                        "p (m t) -> p m t", m=8)
                    vq = vs2[:, 0:1024].rearrange("p (m t) -> p m t", m=8)
                    bcb = sc2bc.unsqueeze(1).broadcast_to([128, 8, 128])
                    v.tensor_tensor(vq, hs, bcb, ALU.mult); D_()
                    v.tensor_scalar(k2[:, q * 1024:(q + 1) * 1024],
                                    vs2[:, 0:1024],
                                    TL["C_t"][:, 0:1], TL["nC_t"][:, 0:1],
                                    op0=ALU.add, op1=ALU.add); D_()
                D_().then_inc(S["k2_"], 1)

            ve_fa1a(0)
            ve_fa1b(0)
            ve_fa2(0)
            ve_fa1a(1)
            ve_fa1b(1)
            ve_fa2(1)
            ve_uev(0)
            ve_scan(0)
            for b in range(NIT + 3):
                oi1 = b - NCH - 1
                oi3 = b - NCH - 3
                if b + 1 < NIT:
                    ve_uev(b + 1)
                    ve_scan(b + 1)
                if 0 <= oi3 < NCH:
                    ve_out(oi3)
                if NCH <= b + 1 < NIT:
                    ve_fb1a(b + 1)
                    ve_fb1b(b + 1)
                if b + 2 < NIT:
                    ve_fa1a(b + 2)
                    ve_fa1b(b + 2)
                if NCH <= b + 1 < NIT:
                    ve_fb2(b + 1)
                if b + 2 < NIT:
                    ve_fa2(b + 2)
                if 0 <= oi1 < NCH:
                    ve_back_a(oi1, b)

        @block.tensor
        def _(t):
            t.wait_ge(S["prep"], 1)
            t.wait_ge(S["k1m"], 1)
            t.matmul(pmisc[:, 0:2], ones1f[:], pack[0:1, 0:2], start=True,
                     stop=True).then_inc(S["k1m"], 1)
            t.wait_ge(S["winc"], 1)
            for a in range(8):
                ti = t.transpose(pT[:, a * 64:(a + 1) * 64],
                                 winB[0:64, a * 128:(a + 1) * 128],
                                 identb[0:64, 0:64])
            ti.then_inc(S["winTd"], 1)
            t.wait_ge(S["wotc"], 1)
            t.wait_ge(S["winTe"], 1)
            for a in range(8):
                ti = t.transpose(pT[0:64, a * 128:(a + 1) * 128],
                                 wqb[0][:, a * 64:(a + 1) * 64], identb[:])
            ti.then_inc(S["wotTd"], 1)
            t.wait_ge(S["wcT1"], 1)
            t.transpose(pmisc[0:1, 128:256], TL["wcol"][:, 0:1],
                        ident32[:]).then_inc(S["wcT1"], 1)
            t.wait_ge(S["sc1m"], 1)
            t.matmul(pmisc[:, 2:4], ones1f[:], pack[0:1, 0:2], start=True,
                     stop=True).then_inc(S["sc1m"], 1)
            t.wait_ge(S["wcT2"], 1)
            t.transpose(pmisc[0:1, 128:256], TL["wcol"][:, 0:1],
                        ident32[:]).then_inc(S["wcT2"], 1)
            t.wait_ge(S["sc2m"], 1)
            t.matmul(pmisc[:, 4:6], ones1f[:], pack[0:1, 0:2], start=True,
                     stop=True).then_inc(S["sc2m"], 1)
            t.wait_ge(S["wotTe"], 1)
            for j in range(NWT):
                t.wait_ge(S["q1"], j + 1)
                if j >= 1:
                    t.wait_ge(S["q1e"], j)
                for f in range(8):
                    ti = t.transpose(pT[:, f * 128:(f + 1) * 128],
                                     wqb[j % 2][:, f * 128:(f + 1) * 128],
                                     identb[:])
                ti.then_inc(S["q1T"], 1)

            def te_k1sT(oi):
                t.wait_ge(S["k1s_"], oi + 1)
                t.wait_ge(S["pTe"], PT_IDX[("k", oi)])
                for f in range(8):
                    ti = t.transpose(pT[:, f * 128:(f + 1) * 128],
                                     k1s[oi % 2][:, f * 128:(f + 1) * 128],
                                     identb[:])
                ti.then_inc(S["k1T"], 1)

            def te_wout(it):
                oi = it - NCH
                t.wait_ge(S["hb"], oi + 1)
                if oi == 0:
                    t.wait_ge(S["woctr"], 1)
                if oi >= 1:
                    t.wait_ge(S["x2_"], oi)
                t.matmul(pssm[:, 0:512], hb_s[:], woct[:, 0:512], start=True,
                         stop=True)
                t.matmul(pssm[:, 512:1024], hb_s[:], woct[:, 512:1024],
                         start=True, stop=True).then_inc(S["wout"], 1)

            def te_mm1(oi, wout_it=None, k1sT_oi=None):
                t.wait_ge(S["k1fm_"], oi + 1)
                if oi == 0:
                    t.wait_ge(S["q1e"], W1_TILES)
                for g in range(8):
                    G = 8 * oi + g
                    bank = pm1a if g % 2 == 0 else pm1b
                    if G >= 2:
                        t.wait_ge(S["gelu"], G - 1)
                    for mi in range(4):
                        m = g * 4 + mi
                        reg = bank[:, mi * 128:(mi + 1) * 128]
                        for k in range(8):
                            ti = t.matmul(
                                reg,
                                t1T[:, k * 4096 + m * 128:k * 4096 + (m + 1) * 128],
                                k1fm[oi % 2][:, k * 128:(k + 1) * 128],
                                start=(k == 0), stop=(k == 7))
                    ti.then_inc(S["mm1"], 1)
                    if g == 0:
                        if wout_it is not None:
                            te_wout(wout_it)
                        if k1sT_oi is not None:
                            te_k1sT(k1sT_oi)

            def te_fa(it):
                t.wait_ge(S["xn_"], it + 1)
                pt_use = PT_IDX[("x", it)]
                if it == 0:
                    t.wait_ge(S["q1e"], NWT)
                else:
                    t.wait_ge(S["pTe"], pt_use)
                for f in range(8):
                    ti = t.transpose(pT[:, f * 128:(f + 1) * 128],
                                     xn[:, f * 128:(f + 1) * 128], identb[:])
                ti.then_inc(S["xnT"], 1)
                t.wait_ge(S["xnfm_"], it + 1)
                if it == 0:
                    t.wait_ge(S["winTe"], 1)
                t.wait_ge(S["uev"], it)
                if it >= NCH + 3:
                    t.wait_ge(S["bce"], it - NCH - 2)
                for k in range(8):
                    ti = t.matmul(pmisc[0:64, 0:128], winT[:, k * 64:(k + 1) * 64],
                                  xnfm[it % 2][:, k * 128:(k + 1) * 128],
                                  start=(k == 0), stop=(k == 7))
                ti.then_inc(S["umm"], 1)

            def te_btr(oi, b):
                t.wait_ge(S["hmax"], oi + 1)
                t.wait_ge(S["uev"], min(b + 2, NIT))
                t.transpose(pmisc[:, 128:256], hred[:],
                            ident32[:]).then_inc(S["hmaxT"], 1)
                t.wait_ge(S["sc2_"], oi + 1)
                t.transpose(pmisc[0:1, 256:384], TL["sc2"][:, 0:1],
                            ident32[:]).then_inc(S["rowT"], 1)
                t.wait_ge(S["rowe"], oi + 1)
                t.matmul(pmisc[:, 384:512], ones1f[:], rowf[:], start=True,
                         stop=True).then_inc(S["bcmm"], 1)

            def te_mm2(oi, tiles):
                if tiles[0] == 0:
                    t.wait_ge(S["k2_"], oi + 1)
                    if oi == 0:
                        t.wait_ge(S["q1e"], NWT)
                    if oi >= 1:
                        t.wait_ge(S["outv"], oi)
                for hk in tiles:
                    for d in range(2):
                        ti = t.matmul(
                            pout[:, d * 512:(d + 1) * 512],
                            k2[:, hk * 128:(hk + 1) * 128],
                            t2T[:, hk * 1024 + d * 512:hk * 1024 + d * 512 + 512],
                            start=(hk == 0), stop=(hk == 31))
                    if hk % 8 == 7:
                        ti.then_inc(S["m2h"], 1)

            te_fa(0)
            for b in range(NIT + 2):
                oi1 = b - NCH - 1
                oi2 = b - NCH - 2
                has1 = 0 <= oi1 < NCH
                has2 = 0 <= oi2 < NCH
                w_it = b + 1 if NCH <= b + 1 < NIT else None
                k_oi = b - NCH if 0 <= b - NCH < NCH else None
                if b + 1 < NIT:
                    te_fa(b + 1)
                if has1:
                    te_mm1(oi1, wout_it=w_it, k1sT_oi=k_oi)
                else:
                    if w_it is not None:
                        te_wout(w_it)
                    if k_oi is not None:
                        te_k1sT(k_oi)
                if has2:
                    te_mm2(oi2, range(0, 16))
                if has1:
                    te_btr(oi1, b)
                if has2:
                    te_mm2(oi2, range(16, 32))

    return nc


def kernel(**inputs):
    x = np.ascontiguousarray(np.asarray(inputs["L0_attn_x"], np.float32))
    xs = x.reshape(NCORE, TOK, D)
    w1 = np.ascontiguousarray(np.asarray(inputs["L0_ffn_w1"], np.float32))
    w2 = np.ascontiguousarray(np.asarray(inputs["L0_ffn_w2"], np.float32))
    win = np.ascontiguousarray(np.asarray(inputs["L0_attn_W_in"], np.float32))
    wout = np.ascontiguousarray(np.asarray(inputs["L0_attn_W_out"], np.float32))
    logA = np.asarray(inputs["L0_attn_log_A"], np.float32).reshape(N, 1)
    Bp = np.asarray(inputs["L0_attn_B_param"], np.float32).reshape(N, 1)
    Cp = np.asarray(inputs["L0_attn_C_param"], np.float32).reshape(N, 1)
    logdt = np.asarray(inputs["L0_attn_log_dt"], np.float32).reshape(1, 1)

    nc = build_nc()
    in_maps = []
    for i in range(NCORE):
        odd = i % 2 == 1
        in_maps.append({
            "xh": xs[i - 1] if odd else xs[i],
            "xo": xs[i],
            "w1": w1, "w2": w2, "win": win, "wout": wout,
            "logA": logA, "Bp": Bp, "Cp": Cp, "logdt": logdt,
            "gate": np.array([[1.0 if odd else 0.0]], np.float32),
        })
    res = run_bass_kernel_spmd(nc, in_maps, core_ids=list(range(NCORE)))
    out = np.concatenate([res.results[i]["out"] for i in range(NCORE)], axis=0)
    return out.reshape(B, L, D)


# version 17
# speedup vs baseline: 1.7385x; 1.0166x over previous
import numpy as np
from contextlib import ExitStack
import concourse.bass as bass
import concourse.mybir as mybir
from concourse.bass_utils import run_bass_kernel_spmd

F32 = mybir.dt.float32
BF16 = mybir.dt.bfloat16
AF = mybir.ActivationFunctionType
ALU = mybir.AluOpType
AX = mybir.AxisListType
C32 = float(1.5 * 2.0**23)
C64 = float(1.5 * 2.0**52)

B, L, D, N, H = 4, 8192, 1024, 64, 4096
NCORE = 8
TOK = L // 2
NCH = TOK // 128
NIT = 2 * NCH
INV_D = 1.0 / D
W1_TILES = 32
W2_TILES = 32
NWT = W1_TILES + W2_TILES

_PT_ORDER = [("x", 0)]
for _b in range(NIT + 2):
    if _b + 1 < NIT:
        _PT_ORDER.append(("x", _b + 1))
    if 0 <= _b - NCH < NCH:
        _PT_ORDER.append(("k", _b - NCH))
PT_IDX = {key: i for i, key in enumerate(_PT_ORDER)}

NP1 = NWT // 2
P1_RING = [[p for p in range(NP1) if p % 3 == r] for r in range(3)]
P1_SLOT = {}
for _r in range(3):
    for _i, _p in enumerate(P1_RING[_r]):
        P1_SLOT[_p] = (_r, _i)
P2_RING = [[j for j in range(NWT) if j % 3 == r] for r in range(3)]
P2_SLOT = {}
for _r in range(3):
    for _i, _j in enumerate(P2_RING[_r]):
        P2_SLOT[_j] = (_r, len(P1_RING[_r]) + _i)


def build_nc():
    nc = bass.Bass("TRN2", target_bir_lowering=False)
    dr = {}
    for n, shp in [("xh", (TOK, D)), ("xo", (TOK, D)),
                   ("w1", (H, D)), ("w2", (D, H)),
                   ("win", (N, D)), ("wout", (D, N)),
                   ("logA", (N, 1)), ("Bp", (N, 1)), ("Cp", (N, 1)),
                   ("logdt", (1, 1)), ("gate", (1, 1))]:
        dr[n] = nc.dram_tensor(n, list(shp), F32, kind="ExternalInput").ap()
    dr["out"] = nc.dram_tensor("out", [TOK, D], F32, kind="ExternalOutput").ap()

    sbt = lambda n, shp, dt=F32: nc.alloc_sbuf_tensor(n, list(shp), dt).ap()
    t1T = sbt("t1T", (128, 8 * H), BF16)
    t2T = sbt("t2T", (128, 32 * D), BF16)
    winT = sbt("winT", (128, 8 * N), BF16)
    woct = sbt("woct", (64, D), BF16)
    xb = [sbt(f"xb{i}", (128, D)) for i in range(2)]
    xn = sbt("xn", (128, D), BF16)
    xnfm = [sbt(f"xnfm{i}", (128, D), BF16) for i in range(2)]
    x2_3 = [sbt(f"x2_{i}", (128, D)) for i in range(3)]
    x2 = None
    xn2 = sbt("xn2", (128, D))
    k1s0 = sbt("k1s0", (128, D), BF16)
    k1s = [k1s0, k1s0]
    k1fm0 = sbt("k1fm0", (128, D), BF16)
    k1fm = [k1fm0, k1fm0]
    h_sb = sbt("h_sb", (128, H))
    vs2 = sbt("vs2", (128, 2048))
    x2 = x2_3 + [vs2[:, 1024:2048]]
    k2 = sbt("k2", (128, H), BF16)
    sc2bc = sbt("sc2bc", (128, 128))
    hred = sbt("hred", (128, 128))
    rowf = sbt("rowf", (1, 128))
    out_s0 = sbt("out_s0", (128, D))
    out_s = [out_s0, out_s0]
    u_s = [sbt(f"u_s{i}", (64, 128)) for i in range(2)]
    h_s = [sbt(f"h_s{i}", (64, 128)) for i in range(2)]
    hb_s = sbt("hb_s", (64, 128), BF16)
    abt = sbt("abt", (64, 128))
    cgt = sbt("cgt", (64, 1))
    zero64 = sbt("zero64", (64, 1))
    identb = sbt("identb", (128, 128), BF16)
    ident32 = sbt("ident32", (128, 128))
    ones1f = sbt("ones1f", (1, 128))
    TLn = ["sm", "sq", "va", "vb", "vc", "sqr", "inv", "nb", "am", "cl", "rr",
           "sc1", "a1t", "am2", "cl2", "r2", "sc2", "a2t", "C_t", "nC_t",
           "one_t", "chi", "c64", "nc64", "r1b", "s1o", "r2b", "s2o", "wcol",
           "vc2", "sqr2", "a2u", "sm2", "sq2u"]
    TLblk = sbt("TLblk", (128, len(TLn)), F32)
    TL = {n: TLblk[:, i:i + 1] for i, n in enumerate(TLn)}
    P64n = ["logA", "Bp", "Cp", "expA", "Aneg", "dtb", "gate64", "lnab",
            "Abar", "am1", "mask", "d1", "e1", "den", "rden", "f0", "f1",
            "f2", "f3", "frac", "Bbar"]
    P64blk = sbt("P64blk", (64, len(P64n)), F32)
    P64 = {n: P64blk[:, i:i + 1] for i, n in enumerate(P64n)}
    sc11 = sbt("sc11", (1, 1))
    sc12 = sbt("sc12", (1, 1))
    gt11 = sbt("gt11", (1, 1))
    pack = sbt("pack", (1, 4))
    wtot = sbt("wtot", (1, 1))
    onesn = sc2bc[0:64, 0:128]
    identD = hred
    k1 = vs2.bitcast(BF16)[:, 2048:3072]
    scr_sq = xn2
    scr_vt = vs2[:, 0:1024]
    scr_sq = xn2
    wst = [vs2[:, 0:1024], vs2[:, 1024:2048]]
    wstB = [h_sb[:, 1024:2048], h_sb[:, 2048:3072]]
    wstC = [h_sb[:, 3072:4096], out_s0[:, 0:1024]]
    WSTG = [wst, wstB, wstC]
    k2f = k2.bitcast(F32)
    WPAIR = [vs2[:, 0:2048], h_sb[:, 1024:3072], k2f[:, 0:2048]]
    wqb = [k2[:, 0:1024], k2[:, 1024:2048]]
    winB = k2[:, 2048:3072]
    woctB0 = k2[:, 3072:4096]
    wost = h_sb[:, 0:512]
    winF = xn2
    wmul = x2_3[0]
    wrnd = x2_3[1]
    wabs = hred
    pT = nc.alloc_psum_tensor("pT", [128, 1024], BF16).ap()
    pmisc = nc.alloc_psum_tensor("pmisc", [128, 512], F32).ap()
    pssm = nc.alloc_psum_tensor("pssm", [128, 1024], F32).ap()
    pm1a = nc.alloc_psum_tensor("pm1a", [128, 512], F32).ap()
    pm1b = nc.alloc_psum_tensor("pm1b", [128, 512], F32).ap()
    pout = nc.alloc_psum_tensor("pout", [128, 1024], F32).ap()

    t1T_kv = t1T.rearrange("p (k r) -> p k r", k=8)
    t2T_qv = t2T.rearrange("p (q f r) -> p q f r", q=4, f=8)
    pT_v8 = pT.rearrange("p (k t) -> p k t", k=8)

    SEMS = ["lx", "lw", "lws", "lwg", "lwt", "lp", "lo", "gp", "prep",
            "wa", "s1b", "s2b", "q1", "q1T", "q1e",
            "zoh1", "zoh2", "k1m", "k1me", "winc", "winTd", "winTe",
            "wotc", "wotTd", "wotTe", "woctr", "wcT1", "wcT2", "sc1m", "sc2m",
            "sq1a", "sq2a", "r1d", "r2d", "sqr1q", "sqr1a", "xn_", "xnT", "xnfm_", "umm", "scan",
            "hb", "wout", "x2_", "sqr2q", "sqr2a", "uev",
            "k1s_", "k1T", "k1fm_", "mm1", "gelu", "hmax", "hmaxT", "sc2_",
            "rowT", "rowe", "bcmm", "bce", "k2_", "m2h", "outv", "od", "pTe"]

    with ExitStack() as ctx:
        S = {n: ctx.enter_context(nc.semaphore(n)) for n in SEMS}
        block = ctx.enter_context(nc.Block())

        WSEM = ["lw", "lws", "lwg"]
        w1_v = dr["w1"].rearrange("(j p) d -> j p d", p=128)
        w2_v = dr["w2"].rearrange("(i p) (q d) -> i q p d", p=128, d=1024)

        def wtile(j):
            return w1_v[j] if j < W1_TILES else \
                w2_v[(j - W1_TILES) // 4, (j - W1_TILES) % 4]

        w1_pv = dr["w1"].rearrange("(a c p) d -> a p c d", c=2, p=128)
        w2_pv = dr["w2"].rearrange("(i p) (b e) -> i b p e", p=128, e=2048)

        def wpairsrc(p):
            if p < W1_TILES // 2:
                return w1_pv[p]
            pp = p - W1_TILES // 2
            return w2_pv[pp // 2, pp % 2].rearrange("p (c d) -> p c d", c=2)

        def w_dma1(eng, r, i):
            p = P1_RING[r][i]
            if i >= 1:
                pprev = P1_RING[r][i - 1]
                eng.wait_ge(S["wa"], pprev + 1)
            if r == 2 and i == 0:
                eng.wait_ge(S["woctr"], 1)
            eng.wait_ge(S[WSEM[r]], 16 * i)
            eng.dma_start(out=WPAIR[r].rearrange("p (c d) -> p c d", c=2),
                          in_=wpairsrc(p)).then_inc(S[WSEM[r]], 16)

        def w_dma(eng, r, t):
            j = t - NWT
            _, si = P2_SLOT[j]
            i2 = si - len(P1_RING[r])
            if i2 < 2:
                eng.wait_ge(S["wa"], NP1)
            else:
                tp = P2_RING[r][i2 - 2]
                eng.wait_ge(S["q1"], tp + 1)
            eng.wait_ge(S[WSEM[r]], 16 * si)
            eng.dma_start(out=WSTG[r][i2 % 2],
                          in_=wtile(j)).then_inc(S[WSEM[r]], 16)

        @block.sync
        def _(sync):
            for di, nm in enumerate(["logA", "Bp", "Cp", "logdt", "gate", "win"]):
                dst = {"logA": P64["logA"], "Bp": P64["Bp"], "Cp": P64["Cp"],
                       "logdt": sc11, "gate": gt11}.get(nm)
                d = winF[0:64, :] if nm == "win" else dst[:]
                sync.wait_ge(S["lp"], 16 * di)
                sync.dma_start(out=d, in_=dr[nm][:, :]).then_inc(S["lp"], 16)
            wo_v = dr["wout"].rearrange("(a p) n -> a p n", p=128)
            for a in range(8):
                sync.wait_ge(S["lo"], 16 * a)
                sync.dma_start(out=wost[:, a * 64:(a + 1) * 64],
                               in_=wo_v[a]).then_inc(S["lo"], 16)
            for i in range(len(P1_RING[1])):
                w_dma1(sync, 1, i)
            for j in P2_RING[1]:
                w_dma(sync, 1, NWT + j)
            for it in range(NIT):
                sync.wait_ge(S["lx"], 16 * it)
                if it >= 2:
                    sync.wait_ge(S["xn_"], it - 1)
                    sync.wait_ge(S["sq1a"], it - 1)
                    if it - 2 >= NCH:
                        sync.wait_ge(S["x2_"], it - 2 - NCH + 1)
                    else:
                        sync.wait_ge(S["scan"], it - 1)
                src = dr["xh"] if it < NCH else dr["xo"]
                c = it if it < NCH else it - NCH
                sync.dma_start(out=xb[it % 2][:],
                               in_=src[c * 128:(c + 1) * 128, :]).then_inc(S["lx"], 16)

        @block.gpsimd
        def _(g):
            g.iota(identD[:], pattern=[[-1, 128]], base=0, channel_multiplier=1,
                   allow_small_or_imprecise_dtypes=True)
            g.drain().then_inc(S["gp"], 1)
            for i in range(len(P1_RING[2])):
                w_dma1(g, 2, i)
            for j in P2_RING[2]:
                w_dma(g, 2, NWT + j)
            def gp_sq2(it):
                oi = it - NCH
                g.wait_ge(S["x2_"], oi + 1)
                if it >= NCH + 1:
                    g.wait_ge(S["k1s_"], it - NCH)
                g.tensor_tensor(scr_sq[:], x2[oi % 4][:], x2[oi % 4][:],
                                ALU.mult)
                g.drain().then_inc(S["sq2a"], 1)

            def gp_sq1(it):
                g.wait_ge(S["lx"], 16 * (it + 1))
                if it >= 1:
                    g.wait_ge(S["r1d"], it)
                if it - NCH >= 1:
                    g.wait_ge(S["r2d"], it - NCH)
                g.tensor_tensor(scr_sq[:], xb[it % 2][:], xb[it % 2][:],
                                ALU.mult)
                g.drain().then_inc(S["sq1a"], 1)

            gp_sq1(0)
            gp_sq1(1)
            for b in range(NIT + 3):
                oi3 = b - NCH - 3
                if NCH <= b + 1 < NIT:
                    gp_sq2(b + 1)
                if b + 2 < NIT:
                    gp_sq1(b + 2)
                if 0 <= oi3 < NCH:
                    g.wait_ge(S["od"], 16 * oi3)
                    g.wait_ge(S["outv"], oi3 + 1)
                    g.dma_start(out=dr["out"][oi3 * 128:(oi3 + 1) * 128, :],
                                in_=out_s[oi3 % 2][:]).then_inc(S["od"], 16)
            g.wait_ge(S["od"], 16 * NCH)

        @block.scalar
        def _(s):
            s.wait_ge(S["lp"], 16 * 6)
            s.activation(P64["expA"][:], P64["logA"][:], AF.Exp, bias=0.0, scale=1.0)
            s.activation(sc12[:], sc11[:], AF.Exp, bias=0.0, scale=1.0)
            s.drain().then_inc(S["zoh1"], 1)
            s.wait_ge(S["k1m"], 2)
            s.copy(P64["dtb"][:], pmisc[0:64, 0:1])
            s.copy(P64["gate64"][:], pmisc[0:64, 1:2])
            s.drain().then_inc(S["k1me"], 1)
            s.wait_ge(S["zoh2"], 1)
            s.activation(P64["Abar"][:], P64["lnab"][:], AF.Exp, bias=0.0, scale=1.0)
            s.drain().then_inc(S["zoh2"], 1)
            s.wait_ge(S["winTd"], 1)
            s.copy(winT[:], pT[:, 0:512])
            s.drain().then_inc(S["winTe"], 1)
            s.wait_ge(S["wotTd"], 1)
            s.copy(woctB0[0:64, :], pT[0:64, 0:1024])
            s.drain().then_inc(S["wotTe"], 1)
            for i in range(len(P1_RING[0])):
                w_dma1(s, 0, i)
            s.wait_ge(S["sc1m"], 2)
            s.copy(TL["r1b"][:], pmisc[:, 2:3])
            s.copy(TL["s1o"][:], pmisc[:, 3:4])
            s.drain().then_inc(S["s1b"], 1)

            def evac_ternary(jm):
                s.wait_ge(S["q1T"], jm + 1)
                if jm < W1_TILES:
                    s.copy(t1T_kv[:, :, jm * 128:(jm + 1) * 128], pT_v8[:])
                else:
                    ii, qq = (jm - W1_TILES) // 4, (jm - W1_TILES) % 4
                    s.copy(t2T_qv[:, qq:qq + 1, :, ii * 128:(ii + 1) * 128],
                           pT_v8.unsqueeze(1))
                s.drain().then_inc(S["q1e"], 1)

            for j in range(NWT):
                if j == W1_TILES:
                    s.wait_ge(S["sc2m"], 2)
                    s.copy(TL["r2b"][:], pmisc[:, 4:5])
                    s.copy(TL["s2o"][:], pmisc[:, 5:6])
                    s.drain().then_inc(S["s2b"], 1)
                if j % 3 == 0:
                    w_dma(s, 0, NWT + j)
                if j >= 1:
                    evac_ternary(j - 1)
            evac_ternary(NWT - 1)

            def se_hb(it):
                s.wait_ge(S["scan"], it + 1)
                s.copy(hb_s[:], h_s[it % 2][:])
                s.drain().then_inc(S["hb"], 1)

            def se_k1fm(oi):
                s.wait_ge(S["k1T"], oi + 1)
                if oi >= 1:
                    s.wait_ge(S["mm1"], 8 * oi)
                s.copy(k1fm[oi % 2][:], pT[:])
                s.drain().then_inc(S["k1fm_"], 1)
                s.drain().then_inc(S["pTe"], 1)

            def se_gelu(oi, g):
                s.wait_ge(S["mm1"], 8 * oi + g + 1)
                if g == 0 and oi >= 1:
                    s.wait_ge(S["k2_"], oi)
                if oi == 0 and g == 2:
                    s.wait_ge(S["q1"], NWT)
                bank = pm1a if g % 2 == 0 else pm1b
                s.activation(h_sb[:, g * 512:(g + 1) * 512], bank[:], AF.Gelu,
                             bias=0.0, scale=1.0)
                s.drain().then_inc(S["gelu"], 1)

            def se_sqrt2(it):
                oi = it - NCH
                s.wait_ge(S["sqr2q"], oi + 1)
                s.activation(TL["sqr2"][:], TL["vc2"][:], AF.Sqrt, bias=0.0,
                             scale=1.0)
                s.drain().then_inc(S["sqr2a"], 1)

            def se_sqrt1(it):
                s.wait_ge(S["sqr1q"], it + 1)
                s.activation(TL["sqr"][:], TL["vc"][:], AF.Sqrt, bias=0.0,
                             scale=1.0)
                s.drain().then_inc(S["sqr1a"], 1)

            def se_xnfm(it):
                s.wait_ge(S["xnT"], it + 1)
                if it >= 2:
                    s.wait_ge(S["umm"], it - 1)
                s.copy(xnfm[it % 2][:], pT[:])
                s.drain().then_inc(S["xnfm_"], 1)
                s.drain().then_inc(S["pTe"], 1)

            def se_rowbc(oi):
                s.wait_ge(S["rowT"], oi + 1)
                s.copy(rowf[:], pmisc[0:1, 256:384])
                s.drain().then_inc(S["rowe"], 1)
                s.wait_ge(S["bcmm"], oi + 1)
                s.copy(sc2bc[:], pmisc[:, 384:512])
                s.drain().then_inc(S["bce"], 1)

            se_sqrt1(0)
            se_sqrt1(1)
            se_xnfm(0)
            for b in range(NIT + 2):
                oi1 = b - NCH - 1
                has1 = 0 <= oi1 < NCH
                if b + 1 < NIT:
                    se_xnfm(b + 1)
                if NCH <= b + 1 < NIT:
                    se_hb(b + 1)
                if has1:
                    for g in range(0, 5):
                        se_gelu(oi1, g)
                if NCH <= b + 1 < NIT:
                    se_sqrt2(b + 1)
                if b + 2 < NIT:
                    se_sqrt1(b + 2)
                if has1:
                    for g in range(5, 8):
                        se_gelu(oi1, g)
                if 0 <= b - NCH < NCH:
                    se_k1fm(b - NCH)
                if has1:
                    se_rowbc(oi1)

        @block.vector
        def _(v):
            D_ = v.drain
            v.wait_ge(S["gp"], 1)
            v.tensor_single_scalar(identb[:], identD[:], 0.0, ALU.is_equal)
            v.tensor_single_scalar(ident32[:], identD[:], 0.0, ALU.is_equal)
            v.memset(ones1f[:], 1.0)
            v.memset(onesn[:], 1.0)
            v.memset(zero64[:], 0.0)
            v.memset(TL["one_t"][:], 1.0)
            v.memset(TL["c64"][:], C64)
            v.memset(TL["nc64"][:], -C64)
            D_()
            v.tensor_scalar(TL["chi"][:], TL["one_t"][:], TL["c64"][:, 0:1],
                            TL["nc64"][:, 0:1], op0=ALU.add, op1=ALU.add); D_()
            v.tensor_scalar(TL["va"][:], TL["chi"][:], TL["c64"][:, 0:1], C32,
                            op0=ALU.mult, op1=ALU.add); D_()
            v.tensor_copy(TL["C_t"][:], TL["va"][:]); D_()
            v.tensor_scalar(TL["nC_t"][:], TL["va"][:], -1.0, None, op0=ALU.mult)
            D_().then_inc(S["prep"], 1)
            v.wait_ge(S["zoh1"], 1)
            v.tensor_scalar(P64["Aneg"][:], P64["expA"][:], -1.0, None,
                            op0=ALU.mult); D_()
            v.tensor_copy(pack[0:1, 0:1], sc12[:])
            v.tensor_copy(pack[0:1, 1:2], gt11[:])
            D_().then_inc(S["k1m"], 1)
            v.wait_ge(S["k1me"], 1)
            v.tensor_tensor(P64["lnab"][:], P64["Aneg"][:], P64["dtb"][:],
                            ALU.mult)
            D_().then_inc(S["zoh2"], 1)
            v.wait_ge(S["zoh2"], 2)
            v.tensor_scalar(P64["am1"][:], P64["Abar"][:], -1.0, None,
                            op0=ALU.add); D_()
            v.tensor_single_scalar(P64["mask"][:], P64["expA"][:], 1e-6,
                                   ALU.is_lt); D_()
            v.tensor_tensor(P64["d1"][:], P64["mask"][:], P64["Aneg"][:],
                            ALU.mult); D_()
            v.tensor_tensor(P64["e1"][:], P64["Aneg"][:], P64["d1"][:],
                            ALU.subtract); D_()
            v.tensor_tensor(P64["den"][:], P64["e1"][:], P64["mask"][:],
                            ALU.add); D_()
            v.reciprocal(P64["rden"][:], P64["den"][:]); D_()
            v.tensor_tensor(P64["f0"][:], P64["am1"][:], P64["rden"][:],
                            ALU.mult); D_()
            v.tensor_tensor(P64["f1"][:], P64["mask"][:], P64["dtb"][:],
                            ALU.mult); D_()
            v.tensor_tensor(P64["f2"][:], P64["mask"][:], P64["f0"][:],
                            ALU.mult); D_()
            v.tensor_tensor(P64["f3"][:], P64["f0"][:], P64["f2"][:],
                            ALU.subtract); D_()
            v.tensor_tensor(P64["frac"][:], P64["f3"][:], P64["f1"][:],
                            ALU.add); D_()
            v.tensor_tensor(P64["Bbar"][:], P64["frac"][:], P64["Bp"][:],
                            ALU.mult); D_()
            v.tensor_single_scalar(abt[:], onesn[:], P64["Abar"][:, 0:1],
                                   ALU.mult); D_()
            v.wait_ge(S["lp"], 16 * 6)
            v.wait_ge(S["lo"], 16 * 8)
            v.tensor_copy(winB[0:64, :], winF[0:64, :])
            D_().then_inc(S["winc"], 1)
            v.tensor_copy(wqb[0][:, 0:512], wost[:, 0:512])
            D_().then_inc(S["wotc"], 1)
            v.wait_ge(S["wotTe"], 1)
            v.tensor_single_scalar(woct[0:64, :], woctB0[0:64, :],
                                   P64["Cp"][:, 0:1], ALU.mult)
            D_().then_inc(S["woctr"], 1)
            for p in range(NP1):
                _r, _si = P1_SLOT[p]
                v.wait_ge(S[WSEM[_r]], 16 * (_si + 1))
                v.tensor_reduce(wabs[:, p:p + 1], WPAIR[_r], AX.X, ALU.add,
                                apply_absolute_value=True)
                D_().then_inc(S["wa"], 1)
                if p == NP1 // 2 - 1:
                    v.tensor_reduce(TL["wcol"][:, 0:1], wabs[:, 0:NP1 // 2],
                                    AX.X, ALU.add)
                    D_().then_inc(S["wcT1"], 1)
                    v.wait_ge(S["wcT1"], 2)
                    v.tensor_reduce(wtot[0:1, 0:1], pmisc[0:1, 128:256], AX.X,
                                    ALU.add); D_()
                    v.tensor_scalar(pack[0:1, 2:3], wtot[0:1, 0:1],
                                    1.0 / (H * D), 1e-5, op0=ALU.mult,
                                    op1=ALU.max); D_()
                    v.reciprocal(pack[0:1, 0:1], pack[0:1, 2:3]); D_()
                    v.tensor_scalar(pack[0:1, 1:2], pack[0:1, 2:3],
                                    1.0 / 127.0, None, op0=ALU.mult)
                    D_().then_inc(S["sc1m"], 1)
            v.wait_ge(S["s1b"], 1)
            v.tensor_reduce(TL["wcol"][:, 0:1], wabs[:, NP1 // 2:NP1],
                            AX.X, ALU.add)
            D_().then_inc(S["wcT2"], 1)
            v.wait_ge(S["wcT2"], 2)
            v.tensor_reduce(wtot[0:1, 0:1], pmisc[0:1, 128:256], AX.X,
                            ALU.add); D_()
            v.tensor_scalar(pack[0:1, 2:3], wtot[0:1, 0:1], 1.0 / (H * D), 1e-5,
                            op0=ALU.mult, op1=ALU.max); D_()
            v.reciprocal(pack[0:1, 0:1], pack[0:1, 2:3]); D_()
            v.tensor_scalar(pack[0:1, 1:2], pack[0:1, 2:3], 1.0 / 127.0, None,
                            op0=ALU.mult)
            D_().then_inc(S["sc2m"], 1)
            v.wait_ge(S["s1b"], 1)
            v.wait_ge(S["wotTd"], 1)
            for j in range(NWT):
                rb = TL["r1b"] if j < W1_TILES else TL["r2b"]
                if j == W1_TILES:
                    v.wait_ge(S["s2b"], 1)
                _r, _si = P2_SLOT[j]
                _i2 = _si - len(P1_RING[_r])
                v.wait_ge(S[WSEM[_r]], 16 * (_si + 1))
                if j >= 2:
                    v.wait_ge(S["q1T"], j - 1)
                v.tensor_single_scalar(wmul[:], WSTG[_r][_i2 % 2], rb[:, 0:1],
                                       ALU.mult); D_()
                v.tensor_scalar(wrnd[:], wmul[:], TL["C_t"][:, 0:1],
                                TL["nC_t"][:, 0:1], op0=ALU.add, op1=ALU.add); D_()
                v.tensor_scalar(wqb[j % 2], wrnd[:], 1.0, -1.0,
                                op0=ALU.min, op1=ALU.max)
                D_().then_inc(S["q1"], 1)

            def ve_scan(it):
                if it == 0:
                    ini = zero64[:, 0:1]
                elif it == NCH:
                    v.tensor_tensor(cgt[:], h_s[(it - 1) % 2][:, 127:128],
                                    P64["gate64"][:], ALU.mult); D_()
                    ini = cgt[:, 0:1]
                else:
                    ini = h_s[(it - 1) % 2][:, 127:128]
                if it - NCH - 1 >= 1:
                    v.wait_ge(S["hb"], it - NCH - 1)
                v.tensor_tensor_scan(h_s[it % 2][:], abt[:], u_s[it % 2][:], ini,
                                     op0=ALU.mult, op1=ALU.add)
                D_().then_inc(S["scan"], 1)

            def ve_out(oi):
                v.wait_ge(S["m2h"], 4 * (oi + 1))
                if oi >= 1:
                    v.wait_ge(S["od"], 16 * oi)
                a2d = TL["a2t"] if oi % 2 == 0 else TL["a2u"]
                for half in range(2):
                    v.scalar_tensor_tensor(out_s[oi % 2][:, half * 512:(half + 1) * 512],
                                           pout[:, half * 512:(half + 1) * 512],
                                           a2d[:, 0:1],
                                           x2[oi % 4][:, half * 512:(half + 1) * 512],
                                           op0=ALU.mult, op1=ALU.add)
                D_().then_inc(S["outv"], 1)

            def ve_fb1a(it):
                oi = it - NCH
                v.wait_ge(S["wout"], oi + 1)
                v.tensor_tensor(x2[oi % 4][:, 0:512], xb[it % 2][:, 0:512],
                                pssm[:, 0:512], ALU.add)
                v.tensor_tensor(x2[oi % 4][:, 512:1024], xb[it % 2][:, 512:1024],
                                pssm[:, 512:1024], ALU.add)
                D_().then_inc(S["x2_"], 1)
                v.tensor_scalar(vs2[:, 0:1024], x2[oi % 4][:], 0.0, None,
                                op0=ALU.add, op1=ALU.add,
                                accum_out=TL["sm2"][:, 0:1]); D_()

            def ve_fb1b(it):
                oi = it - NCH
                v.wait_ge(S["sq2a"], oi + 1)
                v.tensor_reduce(TL["sq2u"][:, 0:1], scr_sq[:], AX.X, ALU.add)
                D_().then_inc(S["r2d"], 1)
                v.tensor_tensor(TL["va"][:], TL["sm2"][:], TL["sm2"][:],
                                ALU.mult); D_()
                v.scalar_tensor_tensor(TL["vb"][:], TL["va"][:], -INV_D,
                                       TL["sq2u"][:], op0=ALU.mult, op1=ALU.add); D_()
                v.tensor_scalar(TL["vc2"][:], TL["vb"][:], INV_D, 1e-5,
                                op0=ALU.mult, op1=ALU.add)
                D_().then_inc(S["sqr2q"], 1)

            def ve_fa1a(it):
                v.wait_ge(S["lx"], 16 * (it + 1))
                v.tensor_scalar(vs2[:, 0:1024], xb[it % 2][:], 0.0, None,
                                op0=ALU.add, op1=ALU.add,
                                accum_out=TL["sm"][:, 0:1]); D_()

            def ve_fa1b(it):
                v.wait_ge(S["sq1a"], it + 1)
                v.tensor_reduce(TL["sq"][:, 0:1], scr_sq[:], AX.X, ALU.add)
                D_().then_inc(S["r1d"], 1)
                v.tensor_tensor(TL["va"][:], TL["sm"][:], TL["sm"][:], ALU.mult); D_()
                v.scalar_tensor_tensor(TL["vb"][:], TL["va"][:], -INV_D,
                                       TL["sq"][:], op0=ALU.mult, op1=ALU.add); D_()
                v.tensor_scalar(TL["vc"][:], TL["vb"][:], INV_D, 1e-5,
                                op0=ALU.mult, op1=ALU.add)
                D_().then_inc(S["sqr1q"], 1)

            def ve_fa2(it):
                v.wait_ge(S["sqr1a"], it + 1)
                if it >= 1:
                    v.wait_ge(S["xnT"], it)
                v.reciprocal(TL["inv"][:], TL["sqr"][:]); D_()
                v.tensor_tensor(TL["vb"][:], TL["sm"][:], TL["inv"][:], ALU.mult); D_()
                v.tensor_scalar(TL["nb"][:], TL["vb"][:], -INV_D, None,
                                op0=ALU.mult); D_()
                v.tensor_scalar(xn[:], xb[it % 2][:], TL["inv"][:, 0:1],
                                TL["nb"][:, 0:1], op0=ALU.mult, op1=ALU.add)
                D_().then_inc(S["xn_"], 1)

            def ve_fb2(it):
                oi = it - NCH
                v.wait_ge(S["sqr2a"], oi + 1)
                v.reciprocal(TL["inv"][:], TL["sqr2"][:]); D_()
                v.tensor_tensor(TL["vb"][:], TL["sm2"][:], TL["inv"][:],
                                ALU.mult); D_()
                v.tensor_scalar(TL["nb"][:], TL["vb"][:], -INV_D, None,
                                op0=ALU.mult); D_()
                if it + 1 < NIT:
                    v.wait_ge(S["sq1a"], it + 2)
                v.tensor_scalar(xn2[:], x2[oi % 4][:], TL["inv"][:, 0:1],
                                TL["nb"][:, 0:1], op0=ALU.mult, op1=ALU.add); D_()
                v.tensor_reduce(TL["am"][:, 0:1], xn2[:], AX.X, ALU.max,
                                apply_absolute_value=True); D_()
                v.tensor_scalar(TL["cl"][:], TL["am"][:], 1e-5, None,
                                op0=ALU.max); D_()
                v.reciprocal(TL["rr"][:], TL["cl"][:]); D_()
                v.tensor_scalar(TL["sc1"][:], TL["rr"][:], 127.0, None,
                                op0=ALU.mult); D_()
                v.tensor_single_scalar(TL["a1t"][:], TL["cl"][:],
                                       TL["s1o"][:, 0:1], ALU.mult); D_()
                v.tensor_scalar(vs2[:, 0:1024], xn2[:], TL["sc1"][:, 0:1],
                                TL["C_t"][:, 0:1], op0=ALU.mult, op1=ALU.add); D_()
                if oi >= 1:
                    v.wait_ge(S["k1T"], oi)
                v.tensor_scalar(k1s[oi % 2][:], vs2[:, 0:1024],
                                TL["nC_t"][:, 0:1], TL["a1t"][:, 0:1],
                                op0=ALU.add, op1=ALU.mult)
                D_().then_inc(S["k1s_"], 1)

            def ve_uev(it):
                v.wait_ge(S["umm"], it + 1)
                v.tensor_single_scalar(u_s[it % 2][:], pmisc[0:64, 0:128],
                                       P64["Bbar"][:, 0:1], ALU.mult)
                D_().then_inc(S["uev"], 1)

            def ve_back_a(oi, b):
                v.wait_ge(S["gelu"], 8 * (oi + 1))
                h3 = h_sb.rearrange("p (m t) -> p t m", m=32)
                v.tensor_reduce(hred[:, 0:128], h3, AX.X, ALU.max,
                                apply_absolute_value=True)
                D_().then_inc(S["hmax"], 1)
                v.wait_ge(S["hmaxT"], oi + 1)
                v.tensor_reduce(TL["am2"][:, 0:1], pmisc[:, 128:256], AX.X,
                                ALU.max); D_()
                v.tensor_scalar(TL["cl2"][:], TL["am2"][:], 1e-5, None,
                                op0=ALU.max); D_()
                v.reciprocal(TL["r2"][:], TL["cl2"][:]); D_()
                v.tensor_scalar(TL["sc2"][:], TL["r2"][:], 127.0, None,
                                op0=ALU.mult); D_()
                a2d = TL["a2t"] if oi % 2 == 0 else TL["a2u"]
                v.tensor_single_scalar(a2d[:], TL["cl2"][:],
                                       TL["s2o"][:, 0:1], ALU.mult)
                D_().then_inc(S["sc2_"], 1)
                v.wait_ge(S["bce"], oi + 1)
                for q in range(4):
                    if oi >= 1:
                        v.wait_ge(S["m2h"], 4 * (oi - 1) + q + 1)
                    hs = h_sb[:, q * 1024:(q + 1) * 1024].rearrange(
                        "p (m t) -> p m t", m=8)
                    vq = vs2[:, 0:1024].rearrange("p (m t) -> p m t", m=8)
                    bcb = sc2bc.unsqueeze(1).broadcast_to([128, 8, 128])
                    v.tensor_tensor(vq, hs, bcb, ALU.mult); D_()
                    v.tensor_scalar(k2[:, q * 1024:(q + 1) * 1024],
                                    vs2[:, 0:1024],
                                    TL["C_t"][:, 0:1], TL["nC_t"][:, 0:1],
                                    op0=ALU.add, op1=ALU.add); D_()
                D_().then_inc(S["k2_"], 1)

            ve_fa1a(0)
            ve_fa1b(0)
            ve_fa2(0)
            ve_fa1a(1)
            ve_fa1b(1)
            ve_fa2(1)
            ve_uev(0)
            ve_scan(0)
            for b in range(NIT + 3):
                oi1 = b - NCH - 1
                oi3 = b - NCH - 3
                if b + 1 < NIT:
                    ve_uev(b + 1)
                    ve_scan(b + 1)
                if 0 <= oi3 < NCH:
                    ve_out(oi3)
                if NCH <= b + 1 < NIT:
                    ve_fb1a(b + 1)
                    ve_fb1b(b + 1)
                if b + 2 < NIT:
                    ve_fa1a(b + 2)
                    ve_fa1b(b + 2)
                if NCH <= b + 1 < NIT:
                    ve_fb2(b + 1)
                if b + 2 < NIT:
                    ve_fa2(b + 2)
                if 0 <= oi1 < NCH:
                    ve_back_a(oi1, b)

        @block.tensor
        def _(t):
            t.wait_ge(S["prep"], 1)
            t.wait_ge(S["k1m"], 1)
            t.matmul(pmisc[:, 0:2], ones1f[:], pack[0:1, 0:2], start=True,
                     stop=True).then_inc(S["k1m"], 1)
            t.wait_ge(S["winc"], 1)
            for a in range(8):
                ti = t.transpose(pT[:, a * 64:(a + 1) * 64],
                                 winB[0:64, a * 128:(a + 1) * 128],
                                 identb[0:64, 0:64])
            ti.then_inc(S["winTd"], 1)
            t.wait_ge(S["wotc"], 1)
            t.wait_ge(S["winTe"], 1)
            for a in range(8):
                ti = t.transpose(pT[0:64, a * 128:(a + 1) * 128],
                                 wqb[0][:, a * 64:(a + 1) * 64], identb[:])
            ti.then_inc(S["wotTd"], 1)
            t.wait_ge(S["wcT1"], 1)
            t.transpose(pmisc[0:1, 128:256], TL["wcol"][:, 0:1],
                        ident32[:]).then_inc(S["wcT1"], 1)
            t.wait_ge(S["sc1m"], 1)
            t.matmul(pmisc[:, 2:4], ones1f[:], pack[0:1, 0:2], start=True,
                     stop=True).then_inc(S["sc1m"], 1)
            t.wait_ge(S["wcT2"], 1)
            t.transpose(pmisc[0:1, 128:256], TL["wcol"][:, 0:1],
                        ident32[:]).then_inc(S["wcT2"], 1)
            t.wait_ge(S["sc2m"], 1)
            t.matmul(pmisc[:, 4:6], ones1f[:], pack[0:1, 0:2], start=True,
                     stop=True).then_inc(S["sc2m"], 1)
            t.wait_ge(S["wotTe"], 1)
            for j in range(NWT):
                t.wait_ge(S["q1"], j + 1)
                if j >= 1:
                    t.wait_ge(S["q1e"], j)
                for f in range(8):
                    ti = t.transpose(pT[:, f * 128:(f + 1) * 128],
                                     wqb[j % 2][:, f * 128:(f + 1) * 128],
                                     identb[:])
                ti.then_inc(S["q1T"], 1)

            def te_k1sT(oi):
                t.wait_ge(S["k1s_"], oi + 1)
                t.wait_ge(S["pTe"], PT_IDX[("k", oi)])
                for f in range(8):
                    ti = t.transpose(pT[:, f * 128:(f + 1) * 128],
                                     k1s[oi % 2][:, f * 128:(f + 1) * 128],
                                     identb[:])
                ti.then_inc(S["k1T"], 1)

            def te_wout(it):
                oi = it - NCH
                t.wait_ge(S["hb"], oi + 1)
                if oi == 0:
                    t.wait_ge(S["woctr"], 1)
                if oi >= 1:
                    t.wait_ge(S["x2_"], oi)
                t.matmul(pssm[:, 0:512], hb_s[:], woct[:, 0:512], start=True,
                         stop=True)
                t.matmul(pssm[:, 512:1024], hb_s[:], woct[:, 512:1024],
                         start=True, stop=True).then_inc(S["wout"], 1)

            def te_mm1(oi, wout_it=None, k1sT_oi=None):
                t.wait_ge(S["k1fm_"], oi + 1)
                if oi == 0:
                    t.wait_ge(S["q1e"], W1_TILES)
                for g in range(8):
                    G = 8 * oi + g
                    bank = pm1a if g % 2 == 0 else pm1b
                    if G >= 2:
                        t.wait_ge(S["gelu"], G - 1)
                    for mi in range(4):
                        m = g * 4 + mi
                        reg = bank[:, mi * 128:(mi + 1) * 128]
                        for k in range(8):
                            ti = t.matmul(
                                reg,
                                t1T[:, k * 4096 + m * 128:k * 4096 + (m + 1) * 128],
                                k1fm[oi % 2][:, k * 128:(k + 1) * 128],
                                start=(k == 0), stop=(k == 7))
                    ti.then_inc(S["mm1"], 1)
                    if g == 0:
                        if wout_it is not None:
                            te_wout(wout_it)
                        if k1sT_oi is not None:
                            te_k1sT(k1sT_oi)

            def te_fa(it):
                t.wait_ge(S["xn_"], it + 1)
                pt_use = PT_IDX[("x", it)]
                if it == 0:
                    t.wait_ge(S["q1e"], NWT)
                else:
                    t.wait_ge(S["pTe"], pt_use)
                for f in range(8):
                    ti = t.transpose(pT[:, f * 128:(f + 1) * 128],
                                     xn[:, f * 128:(f + 1) * 128], identb[:])
                ti.then_inc(S["xnT"], 1)
                t.wait_ge(S["xnfm_"], it + 1)
                if it == 0:
                    t.wait_ge(S["winTe"], 1)
                t.wait_ge(S["uev"], it)
                if it >= NCH + 3:
                    t.wait_ge(S["bce"], it - NCH - 2)
                for k in range(8):
                    ti = t.matmul(pmisc[0:64, 0:128], winT[:, k * 64:(k + 1) * 64],
                                  xnfm[it % 2][:, k * 128:(k + 1) * 128],
                                  start=(k == 0), stop=(k == 7))
                ti.then_inc(S["umm"], 1)

            def te_btr(oi, b):
                t.wait_ge(S["hmax"], oi + 1)
                t.wait_ge(S["uev"], min(b + 2, NIT))
                t.transpose(pmisc[:, 128:256], hred[:],
                            ident32[:]).then_inc(S["hmaxT"], 1)
                t.wait_ge(S["sc2_"], oi + 1)
                t.transpose(pmisc[0:1, 256:384], TL["sc2"][:, 0:1],
                            ident32[:]).then_inc(S["rowT"], 1)
                t.wait_ge(S["rowe"], oi + 1)
                t.matmul(pmisc[:, 384:512], ones1f[:], rowf[:], start=True,
                         stop=True).then_inc(S["bcmm"], 1)

            def te_mm2(oi, tiles):
                if tiles[0] == 0:
                    t.wait_ge(S["k2_"], oi + 1)
                    if oi == 0:
                        t.wait_ge(S["q1e"], NWT)
                    if oi >= 1:
                        t.wait_ge(S["outv"], oi)
                for hk in tiles:
                    for d in range(2):
                        ti = t.matmul(
                            pout[:, d * 512:(d + 1) * 512],
                            k2[:, hk * 128:(hk + 1) * 128],
                            t2T[:, hk * 1024 + d * 512:hk * 1024 + d * 512 + 512],
                            start=(hk == 0), stop=(hk == 31))
                    if hk % 8 == 7:
                        ti.then_inc(S["m2h"], 1)

            te_fa(0)
            for b in range(NIT + 2):
                oi1 = b - NCH - 1
                oi2 = b - NCH - 2
                has1 = 0 <= oi1 < NCH
                has2 = 0 <= oi2 < NCH
                w_it = b + 1 if NCH <= b + 1 < NIT else None
                k_oi = b - NCH if 0 <= b - NCH < NCH else None
                if b + 1 < NIT:
                    te_fa(b + 1)
                if has1:
                    te_mm1(oi1, wout_it=w_it, k1sT_oi=k_oi)
                else:
                    if w_it is not None:
                        te_wout(w_it)
                    if k_oi is not None:
                        te_k1sT(k_oi)
                if has2:
                    te_mm2(oi2, range(0, 16))
                if has1:
                    te_btr(oi1, b)
                if has2:
                    te_mm2(oi2, range(16, 32))

    return nc


def kernel(**inputs):
    x = np.ascontiguousarray(np.asarray(inputs["L0_attn_x"], np.float32))
    xs = x.reshape(NCORE, TOK, D)
    w1 = np.ascontiguousarray(np.asarray(inputs["L0_ffn_w1"], np.float32))
    w2 = np.ascontiguousarray(np.asarray(inputs["L0_ffn_w2"], np.float32))
    win = np.ascontiguousarray(np.asarray(inputs["L0_attn_W_in"], np.float32))
    wout = np.ascontiguousarray(np.asarray(inputs["L0_attn_W_out"], np.float32))
    logA = np.asarray(inputs["L0_attn_log_A"], np.float32).reshape(N, 1)
    Bp = np.asarray(inputs["L0_attn_B_param"], np.float32).reshape(N, 1)
    Cp = np.asarray(inputs["L0_attn_C_param"], np.float32).reshape(N, 1)
    logdt = np.asarray(inputs["L0_attn_log_dt"], np.float32).reshape(1, 1)

    nc = build_nc()
    in_maps = []
    for i in range(NCORE):
        odd = i % 2 == 1
        in_maps.append({
            "xh": xs[i - 1] if odd else xs[i],
            "xo": xs[i],
            "w1": w1, "w2": w2, "win": win, "wout": wout,
            "logA": logA, "Bp": Bp, "Cp": Cp, "logdt": logdt,
            "gate": np.array([[1.0 if odd else 0.0]], np.float32),
        })
    res = run_bass_kernel_spmd(nc, in_maps, core_ids=list(range(NCORE)))
    out = np.concatenate([res.results[i]["out"] for i in range(NCORE)], axis=0)
    return out.reshape(B, L, D)
